# version 17
# baseline (speedup 1.0000x reference)
import math
import numpy as np
import ml_dtypes
import concourse.bass as bass
import concourse.mybir as mybir
from concourse.bass_utils import run_bass_kernel_spmd
from contextlib import ExitStack

F32 = mybir.dt.float32
BF16 = mybir.dt.bfloat16
AF = mybir.ActivationFunctionType
ALU = mybir.AluOpType
AX = mybir.AxisListType

B, T, D = 2, 4096, 2048
H = 16
HPC = 4
NCORE = 8
ROWS = 1024
NT = T // 128
NR = ROWS // 128
CH = 512
DFF = 6144
EPS = 1e-5
LAMBDA_INIT = 0.8 - 0.6 * math.exp(-0.3 * 12)
NEG = -1.0e30


def build_nc(stage_limit=99):
    nc = bass.Bass("TRN2", target_bir_lowering=False)

    ei = lambda n, s: nc.dram_tensor(n, s, F32, kind="ExternalInput")
    x_e = ei("x", [T, D])
    wq_e = ei("wq", [D, CH]); wk_e = ei("wk", [D, CH]); wv_e = ei("wv", [D, CH])
    wout_e = ei("wout", [D, D])
    win_e = ei("w_in", [D, DFF]); bin_e = ei("b_in", [1, DFF])
    woutb_e = ei("w_out_b", [D, D]); bout_e = ei("b_out", [1, D])
    lnw_e = ei("ln_w", [1, D]); lnb_e = ei("ln_b", [1, D])
    wffn_e = ei("w_ffn_in", [D, 136]); wup_e = ei("w_up", [128, D])
    rmsw_e = ei("rms_w", [1, 128])
    lam_e = ei("lam", [1, 256])
    ropea_e = ei("ropea", [T, 1024])
    ropeb_e = ei("ropeb", [ROWS, 2048])
    ctil_e = nc.dram_tensor("ctiles", [32, 128, 256], BF16, kind="ExternalInput")
    mask_e = ei("masks", [128, 4 * 512])
    xr_e = ei("xr", [ROWS, D])
    ident_e = ei("ident", [128, 128])
    out_e = nc.dram_tensor("out", [ROWS, D], F32, kind="ExternalOutput")

    qsp = nc.dram_tensor("qsp", [CH, T], F32)
    ksp = nc.dram_tensor("ksp", [CH, T], F32)
    vsp = nc.dram_tensor("vsp", [T, CH], F32)
    agi = nc.dram_tensor("agi", [4, CH, 1024], F32)
    ago = nc.dram_tensor("ago", [4, D, 1024], F32)
    ari = nc.dram_tensor("ari", [1, D], F32)
    aro = nc.dram_tensor("aro", [1, D], F32)
    xres = nc.dram_tensor("xres", [ROWS, D], F32)
    wq_h = nc.dram_tensor("wq_h", [D, CH], BF16); wq_l = nc.dram_tensor("wq_l", [D, CH], BF16)
    wk_h = nc.dram_tensor("wk_h", [D, CH], BF16); wk_l = nc.dram_tensor("wk_l", [D, CH], BF16)
    wv_h = nc.dram_tensor("wv_h", [D, CH], BF16); wv_l = nc.dram_tensor("wv_l", [D, CH], BF16)
    wo_h = nc.dram_tensor("wo_h", [D, D], BF16); wo_l = nc.dram_tensor("wo_l", [D, D], BF16)
    wi_h = nc.dram_tensor("wi_h", [D, DFF], BF16); wi_l = nc.dram_tensor("wi_l", [D, DFF], BF16)
    wb_h = nc.dram_tensor("wb_h", [D, D], BF16); wb_l = nc.dram_tensor("wb_l", [D, D], BF16)
    wf_h = nc.dram_tensor("wf_h", [D, 136], BF16); wf_l = nc.dram_tensor("wf_l", [D, 136], BF16)
    wu_h = nc.dram_tensor("wu_h", [128, D], BF16); wu_l = nc.dram_tensor("wu_l", [128, D], BF16)

    with ExitStack() as g:
        sb = lambda n, s: g.enter_context(nc.sbuf_tensor(n, s, F32))
        ident = sb("identT", [128, 128])
        lamneg = sb("lamneg", [128, 1])
        sc0 = sb("sc0", [1, 4])

        sA = g.enter_context(nc.semaphore("sA"))
        dA = g.enter_context(nc.semaphore("dA"))
        with nc.Block() as blk, nc.psum_tensor("pA", [128, 4], F32) as pA, \
             nc.sbuf_tensor("lamv", [1, 256], F32) as lamv, \
             nc.sbuf_tensor("ones1", [1, 128], F32) as ones1:
            @blk.sync
            def _(sync):
                sync.dma_start(out=ident[:, :], in_=ident_e[:, :]).then_inc(dA, 16)
                sync.dma_start(out=lamv[:, :], in_=lam_e[:, :]).then_inc(dA, 16)

            @blk.vector
            def _(v):
                v.wait_ge(dA, 32)
                v.memset(ones1[:, :], 1.0)
                v.tensor_tensor(lamv[:, 0:64], lamv[:, 0:64], lamv[:, 64:128], op=ALU.mult)
                v.tensor_tensor(lamv[:, 128:192], lamv[:, 128:192], lamv[:, 192:256], op=ALU.mult)
                v.drain()
                v.reduce_sum(sc0[:, 0:1], lamv[:, 0:64], axis=AX.X)
                v.reduce_sum(sc0[:, 1:2], lamv[:, 128:192], axis=AX.X)
                v.drain()
                v.nop().then_inc(sA, 1)

            @blk.scalar
            def _(s):
                s.wait_ge(sA, 1)
                s.activation(sc0[:, 0:2], sc0[:, 0:2], AF.Exp)
                s.drain()
                s.nop().then_inc(sA, 1)

            @blk.vector
            def _(v):
                v.wait_ge(sA, 2)
                v.tensor_tensor(sc0[:, 2:3], sc0[:, 1:2], sc0[:, 0:1], op=ALU.subtract)
                v.drain()
                v.tensor_scalar(sc0[:, 2:3], sc0[:, 2:3], -float(LAMBDA_INIT), None, op0=ALU.add)
                v.drain()
                v.nop().then_inc(sA, 1)

            @blk.tensor
            def _(t):
                t.wait_ge(sA, 3)
                t.matmul(pA[:, 0:1], ones1[:, :], sc0[:, 2:3], start=True, stop=True).then_inc(sA, 1)

            @blk.vector
            def _(v):
                v.wait_ge(sA, 4)
                v.tensor_copy(lamneg[:, :], pA[:, 0:1])

        late_jobs = [
            (wout_e, wo_h, wo_l, D, D), (win_e, wi_h, wi_l, D, DFF),
            (woutb_e, wb_h, wb_l, D, D), (wffn_e, wf_h, wf_l, D, 136),
            (wup_e, wu_h, wu_l, 128, D),
        ]

        phB = ExitStack()
        if stage_limit >= 1:
            wsb_h = [phB.enter_context(nc.sbuf_tensor(f"wsbh{n}", [128, 16, 512], BF16))
                     for n in "qkv"]
            wsb_l = [phB.enter_context(nc.sbuf_tensor(f"wsbl{n}", [128, 16, 512], BF16))
                     for n in "qkv"]
            _stage_split(nc, g, [wq_e, wk_e, wv_e], wsb_h, wsb_l)
            _stage_b(nc, g, x_e, ropea_e, (wsb_h, wsb_l), ident, qsp, ksp, vsp)
        phB.close()

        ph1 = ExitStack()
        if stage_limit >= 2:
            attn_c = ph1.enter_context(nc.sbuf_tensor("attn_c", [128, NT, CH], F32))
            _stage_c(nc, g, qsp, ksp, vsp, mask_e, lamneg, attn_c, late_jobs)

        if stage_limit >= 3:
            e_cc = g.enter_context(nc.semaphore("e_cc"))
            _stage_d(nc, g, attn_c, ctil_e, rmsw_e, ident, agi, ago, e_cc)
        ph1.close()

        if stage_limit >= 5:
            _stage_f(nc, g, ago, (wo_h, wo_l), xr_e, xres, e_cc)

        ffn_sems = None
        if stage_limit >= 6:
            ffn_sems = _mk_ffn_sems(nc, g)
            _stage_ffn(nc, ffn_sems, 0, xres, (wf_h, wf_l, wu_h, wu_l), ident)

        if stage_limit >= 7:
            _stage_h(nc, g, xres, (wi_h, wi_l, wb_h, wb_l), bin_e, bout_e, lnw_e,
                     lnb_e, ropeb_e, ident, ari, aro)

        if stage_limit >= 8:
            _stage_ffn(nc, ffn_sems, 1, xres, (wf_h, wf_l, wu_h, wu_l), ident)

        if stage_limit >= 9:
            with nc.Block() as blk:
                @blk.sync
                def _(sync):
                    sync.dma_start(out=out_e[:, :], in_=xres[:, :]).then_inc(dA, 16)
                    sync.wait_ge(dA, 48)

    return nc



def _stage_split(nc, g, srcs, wsb_h, wsb_l):
    tiles = [(se, w, t) for w, se in enumerate(srcs) for t in range(16)]
    with ExitStack() as st:
        f32t = [st.enter_context(nc.sbuf_tensor(f"sp_f{i}", [128, CH], F32))
                for i in range(2)]
        d_in = [g.enter_context(nc.semaphore(f"sp_di{i}")) for i in range(2)]
        s_hi = g.enter_context(nc.semaphore("sp_hi"))
        v_lo = g.enter_context(nc.semaphore("sp_lo"))
        with nc.Block() as blk:
            @blk.sync
            def _(sync):
                for i, (se, w, t) in enumerate(tiles):
                    if i >= 2:
                        sync.wait_ge(v_lo, i - 1)
                    sync.dma_start(out=f32t[i % 2][:, :],
                                   in_=se[t * 128:(t + 1) * 128, :]).then_inc(d_in[i % 2], 16)

            @blk.scalar
            def _(s_):
                for i, (se, w, t) in enumerate(tiles):
                    s_.wait_ge(d_in[i % 2], 16 * (i // 2 + 1))
                    s_.copy(wsb_h[w][:, t, :], f32t[i % 2][:, :]).then_inc(s_hi, 1)

            @blk.vector
            def _(v):
                for i, (se, w, t) in enumerate(tiles):
                    v.wait_ge(s_hi, i + 1)
                    v.tensor_tensor(wsb_l[w][:, t, :], f32t[i % 2][:, :],
                                    wsb_h[w][:, t, :],
                                    op=ALU.subtract).then_inc(v_lo, 1)


def _stage_b(nc, g, x_e, ropea_e, wsplit, ident, qsp, ksp, vsp):
    with ExitStack() as st:
        sb = lambda n, s: st.enter_context(nc.sbuf_tensor(n, s, F32))
        ps = lambda n, s: st.enter_context(nc.psum_tensor(n, s, F32))
        sem = lambda n: g.enter_context(nc.semaphore(n))

        xt = [sb(f"xt{i}", [128, D]) for i in range(2)]
        rt = [sb(f"rt{i}", [128, 1024]) for i in range(2)]
        rc = [sb(f"rc{i}", [128, 512]) for i in range(2)]
        lnt = [sb(f"lnt{i}", [128, D]) for i in range(2)]
        lnf_h = [st.enter_context(nc.sbuf_tensor(f"lnfh{i}", [128, 16, 128], BF16))
                 for i in range(2)]
        lnf_l = [st.enter_context(nc.sbuf_tensor(f"lnfl{i}", [128, 16, 128], BF16))
                 for i in range(2)]
        bnst = sb("bnst", [128, 24])
        mv = sb("mv", [128, 2])
        rstd = sb("rstd", [128, 1])
        rtmp = sb("rtmp", [128, 256])
        wsb_h, wsb_l = wsplit
        qr = [sb(f"qr{i}", [128, 512]) for i in range(2)]
        kr = [sb(f"kr{i}", [128, 512]) for i in range(2)]
        vt = [sb(f"vt{i}", [128, 512]) for i in range(2)]
        qkf = [sb(f"qkf{i}", [128, 1024]) for i in range(2)]
        ptp = [ps(f"ptp{i}", [128, 128]) for i in range(2)]
        pq = [ps(f"pq{i}", [128, 512]) for i in range(2)]
        pk = [ps(f"pk{i}", [128, 512]) for i in range(2)]
        pv = [ps(f"pv{i}", [128, 512]) for i in range(2)]

        b_cast = sem("b_cast")
        d_x = [sem("b_dx0"), sem("b_dx1")]
        d_r = [sem("b_dr0"), sem("b_dr1")]
        d_sp = [sem("b_dsp0"), sem("b_dsp1")]
        b_vst = sem("b_vst"); b_act = sem("b_act"); b_ln = sem("b_ln")
        b_tp = sem("b_tp"); b_etp = sem("b_etp"); b_prj = sem("b_prj")
        b_ev = sem("b_ev"); b_qkf = sem("b_qkf")

        with nc.Block() as blk:
            @blk.sync
            def _(sync):
                for i in range(NT):
                    if i >= 2:
                        sync.wait_ge(b_ln, i - 1)
                        sync.wait_ge(b_ev, i - 1)
                    sync.dma_start(out=xt[i % 2][:, :],
                                   in_=x_e[i * 128:(i + 1) * 128, :]).then_inc(d_x[i % 2], 16)
                    sync.dma_start(out=rt[i % 2][:, :],
                                   in_=ropea_e[i * 128:(i + 1) * 128, :]).then_inc(d_r[i % 2], 16)
                    if i >= 1:
                        _emit_spill(sync, i - 1, qkf, vt, qsp, ksp, vsp, b_qkf, d_sp)
                _emit_spill(sync, NT - 1, qkf, vt, qsp, ksp, vsp, b_qkf, d_sp)
                sync.wait_ge(d_sp[0], 144 * (NT // 2))
                sync.wait_ge(d_sp[1], 144 * (NT // 2))

            @blk.scalar
            def _(s):
                for i in range(NT):
                    s.wait_ge(b_vst, i + 1)
                    s.activation(rstd[:, :], rstd[:, :], AF.Ln)
                    s.drain()
                    s.activation(rstd[:, :], rstd[:, :], AF.Exp, scale=-0.5).then_inc(b_act, 1)
                    for c in range(16):
                        gt = 24 * i + c
                        s.wait_ge(b_tp, gt + 1)
                        s.copy(lnf_h[i % 2][:, c, :], ptp[gt % 2][:, :]).then_inc(b_cast, 1)

            @blk.vector
            def _(v):
                for i in range(NT):
                    bb = i % 2
                    v.wait_ge(d_x[i % 2], 16 * (i // 2 + 1))
                    for c in range(4):
                        v.bn_stats(bnst[:, 6 * c:6 * (c + 1)], xt[bb][:, 512 * c:512 * (c + 1)])
                    v.drain()
                    v.bn_aggr(mv[:, :], bnst[:, :])
                    v.drain()
                    v.tensor_scalar(rstd[:, :], mv[:, 1:2], EPS, None,
                                    op0=ALU.add).then_inc(b_vst, 1)
                    v.wait_ge(b_act, i + 1)
                    v.tensor_scalar(lnt[bb][:, :], xt[bb][:, :], mv[:, 0:1], rstd[:, 0:1],
                                    op0=ALU.subtract, op1=ALU.mult).then_inc(b_ln, 1)
                    for c in range(16):
                        gt = 24 * i + c
                        v.wait_ge(b_cast, 16 * i + c + 1)
                        v.tensor_tensor(lnf_l[bb][:, c, :], ptp[gt % 2][:, :],
                                        lnf_h[bb][:, c, :],
                                        op=ALU.subtract).then_inc(b_etp, 1)
                    v.wait_ge(d_r[i % 2], 16 * (i // 2 + 1))
                    v.tensor_tensor(rc[bb][:, :], rt[bb][:, 0:512], rt[bb][:, 512:1024], op=ALU.add)
                    v.drain()
                    if i >= 2:
                        v.wait_ge(d_sp[i % 2], 144 * (i // 2))
                    for widx, (psrc, dst) in enumerate(((pq, qr), (pk, kr))):
                        v.wait_ge(b_prj, 3 * i + widx + 1)
                        src = psrc[bb]
                        cos, sin = rc[bb][:, 0:256], rc[bb][:, 256:512]
                        ev, od = src[:, 0::2], src[:, 1::2]
                        de, do = dst[bb][:, 0::2], dst[bb][:, 1::2]
                        v.tensor_tensor(de, ev, cos, op=ALU.mult)
                        v.tensor_tensor(do, od, cos, op=ALU.mult)
                        v.tensor_tensor(rtmp[:, :], od, sin, op=ALU.mult)
                        v.drain()
                        v.tensor_tensor(de, de, rtmp[:, :], op=ALU.subtract)
                        v.drain()
                        v.tensor_tensor(rtmp[:, :], ev, sin, op=ALU.mult)
                        v.drain()
                        v.tensor_tensor(do, do, rtmp[:, :], op=ALU.add)
                        v.drain()
                    v.wait_ge(b_prj, 3 * i + 3)
                    v.tensor_copy(vt[bb][:, :], pv[bb][:, :]).then_inc(b_ev, 1)
                    for c in range(8):
                        gt = 24 * i + 16 + c
                        v.wait_ge(b_tp, gt + 1)
                        v.tensor_copy(qkf[bb][:, c * 128:(c + 1) * 128],
                                      ptp[gt % 2][:, :]).then_inc(b_etp, 1)
                    v.drain()
                    v.nop().then_inc(b_qkf, 1)

            @blk.tensor
            def _(t):
                for i in range(NT):
                    bb = i % 2
                    t.wait_ge(b_ln, i + 1)
                    for c in range(16):
                        gt = 24 * i + c
                        if gt >= 2:
                            t.wait_ge(b_etp, gt - 1)
                        t.transpose(ptp[gt % 2][:, :], lnt[bb][:, c * 128:(c + 1) * 128],
                                    ident[:, :]).then_inc(b_tp, 1)
                    t.wait_ge(b_etp, 24 * i + 16)
                    if i >= 2:
                        t.wait_ge(b_ev, i - 1)
                    for wi_, pdst in ((0, pq), (1, pk), (2, pv)):
                        n_mm = 0
                        for c in range(16):
                            for xs_, ws_ in ((lnf_h[bb], wsb_h[wi_]), (lnf_l[bb], wsb_h[wi_]),
                                             (lnf_h[bb], wsb_l[wi_])):
                                mm = t.matmul(pdst[bb][:, :], xs_[:, c, :], ws_[:, c, :],
                                              start=(n_mm == 0), stop=(n_mm == 47))
                                n_mm += 1
                        mm.then_inc(b_prj, 1)
                    t.wait_ge(b_ev, i + 1)
                    for c in range(8):
                        gt = 24 * i + 16 + c
                        if gt >= 2:
                            t.wait_ge(b_etp, gt - 1)
                        src = qr[bb] if c < 4 else kr[bb]
                        t.transpose(ptp[gt % 2][:, :], src[:, (c % 4) * 128:((c % 4) + 1) * 128],
                                    ident[:, :]).then_inc(b_tp, 1)


def _emit_spill(sync, i, qkf, vt, qsp, ksp, vsp, b_qkf, d_sp):
    bb = i % 2
    sync.wait_ge(b_qkf, i + 1)
    for j in range(4):
        sync.dma_start(out=qsp[j * 128:(j + 1) * 128, i * 128:(i + 1) * 128],
                       in_=qkf[bb][:, j * 128:(j + 1) * 128]).then_inc(d_sp[bb], 16)
        sync.dma_start(out=ksp[j * 128:(j + 1) * 128, i * 128:(i + 1) * 128],
                       in_=qkf[bb][:, 512 + j * 128:512 + (j + 1) * 128]).then_inc(d_sp[bb], 16)
    sync.dma_start(out=vsp[i * 128:(i + 1) * 128, :], in_=vt[bb][:, :]).then_inc(d_sp[bb], 16)


def _stage_c(nc, g, qsp, ksp, vsp, mask_e, lamneg, attn_c, late_jobs):
    NSP = 16
    sched = []
    for h in range(HPC):
        for j in range(NSP):
            for s in range(2 * j + 2):
                sched.append((h, j, s))
    idx = {t: n + 1 for n, t in enumerate(sched)}
    span_idx = {}
    n = 0
    for h in range(HPC):
        for j in range(NSP):
            n += 1
            span_idx[(h, j)] = n

    bmask_cnt = {}
    nb = 0
    for (h, j, s) in sched:
        if s >= 2 * j:
            nb += 1
            bmask_cnt[(h, j, s)] = nb

    with ExitStack() as st:
        sb = lambda n_, s_: st.enter_context(nc.sbuf_tensor(n_, s_, F32))
        ps = lambda n_, s_: st.enter_context(nc.psum_tensor(n_, s_, F32))
        sem = lambda n_: g.enter_context(nc.semaphore(n_))

        kt = [sb(f"c_kt{i}", [128, T]) for i in range(2)]
        qt = [sb(f"c_qt{i}", [128, T]) for i in range(2)]
        vsb = [sb(f"c_v{i}", [128, NT, 132]) for i in range(2)]
        e1 = [sb(f"c_e1{i}", [128, 256]) for i in range(2)]
        e2 = [sb(f"c_e2{i}", [128, 256]) for i in range(2)]
        msk = sb("c_msk", [128, 512])
        rz = sb("c_rz", [128, 4])
        tm1 = sb("c_tm1", [128, 128])
        tm2 = sb("c_tm2", [128, 128])
        sc = [[ps(f"c_sc{i}{p}", [128, 256]) for p in range(2)] for i in range(2)]
        av = [ps(f"c_av{i}", [128, 132]) for i in range(4)]

        d_qk = [sem("c_dqk0"), sem("c_dqk1")]
        d_v = [sem("c_dv0"), sem("c_dv1")]
        d_m = sem("c_dm")
        t_sc = sem("c_tsc"); t_av = sem("c_tav")
        a_exp = sem("c_aexp")
        v_msk = sem("c_vmsk"); v_avev = sem("c_avev"); v_ones = sem("c_vones")

        SPC = 2048
        tiles2 = []
        for (se_, he_, le_, rows_, cols_) in late_jobs:
            for r0_ in range(0, rows_, 128):
                for c0_ in range(0, cols_, SPC):
                    tiles2.append((se_, he_, le_, r0_, c0_, min(SPC, cols_ - c0_)))
        f2 = [st.enter_context(nc.sbuf_tensor(f"c_spf{i}", [128, SPC], F32))
              for i in range(2)]
        h2 = [st.enter_context(nc.sbuf_tensor(f"c_sph{i}", [128, SPC], BF16))
              for i in range(2)]
        l2 = [st.enter_context(nc.sbuf_tensor(f"c_spl{i}", [128, SPC], BF16))
              for i in range(2)]
        sd = [sem("c_spd0"), sem("c_spd1")]

        with nc.Block() as blk:
            @blk.gpsimd
            def _(gp):
                nlt = len(tiles2)
                for i in range(min(2, nlt)):
                    se_, he_, le_, r0_, c0_, cw_ = tiles2[i]
                    gp.dma_start(out=f2[i % 2][:, 0:cw_],
                                 in_=se_[r0_:r0_ + 128, c0_:c0_ + cw_]).then_inc(sd[i % 2], 16)
                for i in range(nlt):
                    b = i % 2
                    k = i // 2
                    se_, he_, le_, r0_, c0_, cw_ = tiles2[i]
                    gp.wait_ge(sd[b], 48 * k + 16)
                    gp.tensor_copy(h2[b][:, 0:cw_], f2[b][:, 0:cw_])
                    gp.drain()
                    gp.tensor_tensor(l2[b][:, 0:cw_], f2[b][:, 0:cw_], h2[b][:, 0:cw_],
                                     op=ALU.subtract)
                    gp.drain()
                    gp.dma_start(out=he_[r0_:r0_ + 128, c0_:c0_ + cw_],
                                 in_=h2[b][:, 0:cw_]).then_inc(sd[b], 16)
                    gp.dma_start(out=le_[r0_:r0_ + 128, c0_:c0_ + cw_],
                                 in_=l2[b][:, 0:cw_]).then_inc(sd[b], 16)
                    if i + 2 < nlt:
                        se2, he2, le2, r20, c20, cw2 = tiles2[i + 2]
                        gp.dma_start(out=f2[b][:, 0:cw2],
                                     in_=se2[r20:r20 + 128, c20:c20 + cw2]).then_inc(sd[b], 16)
                gp.wait_ge(sd[0], 48 * ((nlt + 1) // 2))
                gp.wait_ge(sd[1], 48 * (nlt // 2))

            @blk.sync
            def _(sync):
                sync.dma_start(out=msk[:, :], in_=mask_e[:, 0:512]).then_inc(d_m, 16)
                for h in range(HPC):
                    hb = h % 2
                    if h >= 2:
                        sync.wait_ge(t_av, 272 * (h - 1))
                    sync.dma_start(out=kt[hb][:, :],
                                   in_=ksp[h * 128:(h + 1) * 128, :]).then_inc(d_qk[hb], 16)
                    sync.dma_start(out=qt[hb][:, :],
                                   in_=qsp[h * 128:(h + 1) * 128, :]).then_inc(d_qk[hb], 16)
                    sync.dma_start(
                        out=bass.AP(vsb[hb], 0, [[132 * NT, 128], [132, NT], [1, 128]]),
                        in_=bass.AP(vsp, h * 128, [[CH, 128], [CH * 128, NT], [1, 128]]),
                    ).then_inc(d_v[hb], 16)

            @blk.vector
            def _(v):
                v.wait_ge(d_m, 16)
                for h in range(HPC):
                    hb = h % 2
                    if h >= 2:
                        v.wait_ge(t_av, 272 * (h - 1))
                    v.memset(vsb[hb][:, :, 128:132], 0.0)
                    v.drain()
                    v.memset(vsb[hb][:, :, 128:129], 1.0)
                    v.drain()
                    v.nop().then_inc(v_ones, 1)
                    for j in range(NSP):
                        for s in (2 * j, 2 * j + 1):
                            d = s - 2 * j
                            v.wait_ge(t_sc, idx[(h, j, s)])
                            v.tensor_tensor(sc[s % 2][0][:, :], sc[s % 2][0][:, :],
                                            msk[:, d * 256:(d + 1) * 256], op=ALU.add)
                            v.tensor_tensor(sc[s % 2][1][:, :], sc[s % 2][1][:, :],
                                            msk[:, d * 256:(d + 1) * 256],
                                            op=ALU.add).then_inc(v_msk, 1)
                        v.wait_ge(t_av, idx[(h, j, 2 * j + 1)])
                        for qb in range(2):
                            v.reciprocal(rz[:, 0:1], av[qb][:, 128:129])
                            v.reciprocal(rz[:, 1:2], av[2 + qb][:, 128:129])
                            v.drain()
                            v.tensor_scalar(tm1[:, :], av[qb][:, 0:128],
                                            rz[:, 0:1], None, op0=ALU.mult)
                            v.tensor_scalar(tm2[:, :], av[2 + qb][:, 0:128],
                                            rz[:, 1:2], None, op0=ALU.mult)
                            v.drain()
                            v.scalar_tensor_tensor(
                                attn_c[:, 2 * j + qb, h * 128:(h + 1) * 128],
                                tm2[:, :], lamneg[:, 0:1], tm1[:, :],
                                op0=ALU.mult, op1=ALU.add)
                        v.drain()
                        v.nop().then_inc(v_avev, 1)

            @blk.scalar
            def _(s_):
                for (h, j, s) in sched:
                    if s >= 2 * j:
                        s_.wait_ge(v_msk, bmask_cnt[(h, j, s)])
                    else:
                        s_.wait_ge(t_sc, idx[(h, j, s)])
                    if idx[(h, j, s)] > 2:
                        s_.wait_ge(t_av, idx[(h, j, s)] - 2)
                    s_.activation(e1[s % 2][:, :], sc[s % 2][0][:, :], AF.Exp, scale=0.125)
                    s_.activation(e2[s % 2][:, :], sc[s % 2][1][:, :], AF.Exp,
                                  scale=0.125).then_inc(a_exp, 1)

            @blk.tensor
            def _(t):
                for h in range(HPC):
                    hb = h % 2
                    t.wait_ge(d_qk[hb], 32 * (h // 2 + 1))
                    t.wait_ge(d_v[hb], 16 * (h // 2 + 1))
                    t.wait_ge(v_ones, h + 1)
                    for j in range(NSP):
                        L = 2 * j + 2
                        for s in range(L):
                            if idx[(h, j, s)] > 2:
                                t.wait_ge(a_exp, idx[(h, j, s)] - 2)
                            t.matmul(sc[s % 2][0][:, :],
                                     kt[hb][0:64, s * 128:(s + 1) * 128],
                                     qt[hb][0:64, j * 256:(j + 1) * 256],
                                     start=True, stop=True, tile_position=(0, 0))
                            t.matmul(sc[s % 2][1][:, :],
                                     kt[hb][64:128, s * 128:(s + 1) * 128],
                                     qt[hb][64:128, j * 256:(j + 1) * 256],
                                     start=True, stop=True,
                                     tile_position=(64, 0)).then_inc(t_sc, 1)
                            if s >= 1:
                                _emit_av(t, h, j, s - 1, L, e1, e2, vsb, av, idx,
                                         span_idx, a_exp, t_av, v_avev, hb)
                        _emit_av(t, h, j, L - 1, L, e1, e2, vsb, av, idx,
                                 span_idx, a_exp, t_av, v_avev, hb)


def _emit_av(t, h, j, s, L, e1, e2, vsb, av, idx, span_idx, a_exp, t_av, v_avev, hb):
    t.wait_ge(a_exp, idx[(h, j, s)])
    if s == 0 and span_idx[(h, j)] > 1:
        t.wait_ge(v_avev, span_idx[(h, j)] - 1)
    mm = None
    for p, e in ((0, e1), (1, e2)):
        for qb in range(2):
            mm = t.matmul(av[2 * p + qb][:, :],
                          e[s % 2][:, qb * 128:(qb + 1) * 128],
                          vsb[hb][:, s, :],
                          start=(s == 0), stop=(s == L - 1))
    mm.then_inc(t_av, 1)


def _stage_d(nc, g, attn_c, ctil_e, rmsw_e, ident, agi, ago, e_cc):
    with ExitStack() as st:
        sb = lambda n_, s_: st.enter_context(nc.sbuf_tensor(n_, s_, F32))
        ps = lambda n_, s_: st.enter_context(nc.psum_tensor(n_, s_, F32))
        sem = lambda n_: g.enter_context(nc.semaphore(n_))

        ctl_raw = st.enter_context(nc.sbuf_tensor("d_ctraw", [128, NT, 256], BF16))
        ach = st.enter_context(nc.sbuf_tensor("d_ach", [128, NT, 512], BF16))
        acl = st.enter_context(nc.sbuf_tensor("d_acl", [128, NT, 512], BF16))
        d_split = g.enter_context(nc.semaphore("d_split"))
        rmsw_t = sb("d_rmsw", [128, 128])
        rw_row = sb("d_rwrow", [1, 128])
        ones_d = sb("d_ones", [1, 128])
        sqd = sb("d_sqd", [128, 4, 128])
        acc4 = sb("d_acc4", [128, 4])
        rs4 = sb("d_rs4", [128, 4])
        af = [sb(f"d_af{i}", [128, 512]) for i in range(2)]
        agf = [sb(f"d_agf{i}", [128, 512]) for i in range(2)]
        cv = [ps(f"d_cv{i}", [128, 512]) for i in range(2)]
        ptp = [ps(f"d_ptp{i}", [128, 128]) for i in range(2)]
        pbc = ps("d_pbc", [128, 128])

        d_ct = sem("d_dct"); d_rm = d_ct
        d_ag = [sem("d_dag0"), sem("d_dag1")]
        s_pr = sem("d_spr")
        t_cv = sem("d_tcv"); a_sq = sem("d_asq"); v_ts = sem("d_vts")
        a_rs = sem("d_ars"); v_af = sem("d_vaf")
        t_tp = sem("d_ttp"); v_etp = sem("d_vetp"); v_agf = sem("d_vagf")

        with nc.Block() as blk:
            @blk.sync
            def _(sync):
                sync.dma_start(
                    out=bass.AP(ctl_raw, 0, [[NT * 256, 128], [256, NT], [1, 256]]),
                    in_=bass.AP(ctil_e, 0, [[256, 128], [128 * 256, NT], [1, 256]]),
                ).then_inc(d_ct, 16)
                sync.dma_start(out=rw_row[:, :], in_=rmsw_e[:, :]).then_inc(d_rm, 16)
                for i in range(NT):
                    sync.wait_ge(v_agf, i + 1)
                    qq, tl = i // 8, i % 8
                    for u in range(4):
                        sync.dma_start(
                            out=bass.AP(agi, qq * CH * 1024 + u * 128 * 1024 + tl * 128,
                                        [[1024, 128], [1, 128]]),
                            in_=agf[i % 2][:, u * 128:(u + 1) * 128],
                        ).then_inc(d_ag[i % 2], 16)
                sync.wait_ge(d_ag[0], 64 * (NT // 2))
                sync.wait_ge(d_ag[1], 64 * (NT // 2))

            @blk.vector
            def _(v):
                v.memset(ones_d[:, :], 1.0)
                v.wait_ge(d_split, 1)
                v.tensor_tensor(bass.AP(acl, 0, [[NT * 512, 128], [1, NT * 512]]),
                                bass.AP(attn_c, 0, [[NT * CH, 128], [1, NT * CH]]),
                                bass.AP(ach, 0, [[NT * 512, 128], [1, NT * 512]]),
                                op=ALU.subtract).then_inc(d_split, 1)
                v.wait_ge(d_rm, 32)
                v.tensor_scalar(rw_row[:, :], rw_row[:, :], 1.0 - float(LAMBDA_INIT),
                                None, op0=ALU.mult).then_inc(s_pr, 1)
                v.wait_ge(s_pr, 2)
                v.tensor_copy(rmsw_t[:, :], pbc[:, :]).then_inc(s_pr, 1)
                for i in range(NT):
                    v.wait_ge(a_sq, i + 1)
                    if i >= 1:
                        v.wait_ge(a_rs, i)
                    v.tensor_scalar(acc4[:, :], acc4[:, :], 1.0 / 128.0, EPS,
                                    op0=ALU.mult, op1=ALU.add).then_inc(v_ts, 1)
                    v.wait_ge(a_rs, i + 1)
                    if i >= 2:
                        v.wait_ge(t_tp, 4 * (i - 1))
                    for hh in range(4):
                        v.scalar_tensor_tensor(af[i % 2][:, hh * 128:(hh + 1) * 128],
                                               cv[i % 2][:, hh * 128:(hh + 1) * 128],
                                               rs4[:, hh:hh + 1], rmsw_t[:, :],
                                               op0=ALU.mult, op1=ALU.mult)
                    v.drain()
                    v.nop().then_inc(v_af, 1)
                    if i >= 2:
                        v.wait_ge(d_ag[i % 2], 64 * (i // 2))
                    for u in range(4):
                        gt = 4 * i + u
                        v.wait_ge(t_tp, gt + 1)
                        v.tensor_copy(agf[i % 2][:, u * 128:(u + 1) * 128],
                                      ptp[gt % 2][:, :]).then_inc(v_etp, 1)
                    v.drain()
                    v.nop().then_inc(v_agf, 1)

            @blk.scalar
            def _(s_):
                s_.copy(bass.AP(ach, 0, [[NT * 512, 128], [1, NT * 512]]),
                        bass.AP(attn_c, 0, [[NT * CH, 128], [1, NT * CH]])).then_inc(d_split, 1)
                for i in range(NT):
                    s_.wait_ge(t_cv, i + 1)
                    if i >= 1:
                        s_.wait_ge(v_ts, i)
                    for hh in range(4):
                        s_.activation(sqd[:, hh, :], cv[i % 2][:, hh * 128:(hh + 1) * 128],
                                      AF.Square, accum_out=acc4[:, hh:hh + 1])
                    s_.drain()
                    s_.nop().then_inc(a_sq, 1)
                    s_.wait_ge(v_ts, i + 1)
                    s_.activation(rs4[:, :], acc4[:, :], AF.Ln)
                    s_.drain()
                    s_.activation(rs4[:, :], rs4[:, :], AF.Exp, scale=-0.5).then_inc(a_rs, 1)

            @blk.tensor
            def _(t):
                t.wait_ge(s_pr, 1)
                t.matmul(pbc[:, :], ones_d[:, :], rw_row[:, :],
                         start=True, stop=True).then_inc(s_pr, 1)
                t.wait_ge(d_ct, 32)
                t.wait_ge(d_split, 2)
                for i in range(NT):
                    if i >= 2:
                        t.wait_ge(v_af, i - 1)
                    mm = None
                    band = [(i - d) % NT for d in (30, 31, 0, 1, 2)]
                    n_mm = 0
                    for S in band:
                        d = (i - S) % NT
                        for cc_, aa_ in ((ctl_raw[:, d, 0:128], ach), (ctl_raw[:, d, 128:256], ach),
                                         (ctl_raw[:, d, 0:128], acl)):
                            mm = t.matmul(cv[i % 2][:, :], cc_, aa_[:, S, :],
                                          start=(n_mm == 0), stop=(n_mm == 3 * len(band) - 1))
                            n_mm += 1
                    mm.then_inc(t_cv, 1)
                    t.wait_ge(v_af, i + 1)
                    for u in range(4):
                        gt = 4 * i + u
                        if gt >= 2:
                            t.wait_ge(v_etp, gt - 1)
                        t.transpose(ptp[gt % 2][:, :], af[i % 2][:, u * 128:(u + 1) * 128],
                                    ident[:, :]).then_inc(t_tp, 1)

            @blk.gpsimd
            def _(gp):
                for qq in range(4):
                    gp.wait_ge(d_ag[0], 64 * (4 * qq + 4))
                    gp.wait_ge(d_ag[1], 64 * (4 * qq + 4))
                    gp.collective_compute(
                        "AllGather", ALU.bypass,
                        replica_groups=[[0, 1, 2, 3], [4, 5, 6, 7]],
                        ins=[bass.AP(agi, qq * CH * 1024, [[1024, CH], [1, 1024]]).opt()],
                        outs=[bass.AP(ago, qq * D * 1024, [[1024, D], [1, 1024]]).opt()],
                    ).then_inc(e_cc, 1)
                gp.wait_ge(e_cc, 4)


def _stage_e(nc, g, agi, ago):
    e_cc = g.enter_context(nc.semaphore("e_cc"))
    with nc.Block() as blk:
        @blk.gpsimd
        def _(gp):
            gp.collective_compute(
                "AllGather", ALU.bypass,
                replica_groups=[[0, 1, 2, 3], [4, 5, 6, 7]],
                ins=[agi.ap().opt()], outs=[ago.ap().opt()],
            ).then_inc(e_cc, 1)
            gp.wait_ge(e_cc, 1)


def _stage_f(nc, g, ago, wsplit, xr_e, xres, e_cc):
    wout_h, wout_l = wsplit
    with ExitStack() as st:
        sb = lambda n_, s_: st.enter_context(nc.sbuf_tensor(n_, s_, F32))
        ps = lambda n_, s_: st.enter_context(nc.psum_tensor(n_, s_, F32))
        sem = lambda n_: g.enter_context(nc.semaphore(n_))

        wo_h = [st.enter_context(nc.sbuf_tensor(f"f_woh{i}", [128, 16, 512], BF16))
                for i in range(2)]
        wo_l = [st.enter_context(nc.sbuf_tensor(f"f_wol{i}", [128, 16, 512], BF16))
                for i in range(2)]
        ag_all = sb("f_agall", [128, 16, ROWS])
        agh = st.enter_context(nc.sbuf_tensor("f_agh", [128, 16, ROWS], BF16))
        agl = st.enter_context(nc.sbuf_tensor("f_agl", [128, 16, ROWS], BF16))
        f_spl = g.enter_context(nc.semaphore("f_spl"))
        xrt = [sb(f"f_xrt{i}", [128, 512]) for i in range(2)]
        xo = [sb(f"f_xo{i}", [128, 512]) for i in range(2)]
        pf = [ps(f"f_pf{i}", [128, 512]) for i in range(2)]

        d_wo = [sem("f_dwo0"), sem("f_dwo1")]
        d_ag = e_cc
        d_xr = [sem("f_dxr0"), sem("f_dxr1")]
        d_xo = [sem("f_dxo0"), sem("f_dxo1")]
        t_mm = sem("f_tmm"); v_ev = sem("f_vev")

        NG = 4 * NR

        with nc.Block() as blk:
            @blk.sync
            def _(sync):
                with sync.register("pid") as pid:
                    sync.reg_mov(pid, sync.partition_id())
                    sync.reg_mod(pid, pid, 4)
                    sync.reg_mul(pid, pid, D * 1024)
                    sync.dma_start(
                        out=ag_all[:, :, :],
                        in_=bass.AP(ago, pid, [[1024, 128], [1024 * 128, 16], [1, ROWS]]),
                    ).then_inc(d_ag, 16)
                for mo in range(4):
                    if mo >= 2:
                        sync.wait_ge(t_mm, NR * (mo - 1))
                    sync.dma_start(
                        out=wo_h[mo % 2][:, :, :],
                        in_=bass.AP(wout_h, mo * 512, [[D, 128], [D * 128, 16], [1, 512]]),
                    ).then_inc(d_wo[mo % 2], 16)
                    sync.dma_start(
                        out=wo_l[mo % 2][:, :, :],
                        in_=bass.AP(wout_l, mo * 512, [[D, 128], [D * 128, 16], [1, 512]]),
                    ).then_inc(d_wo[mo % 2], 16)
                    for r in range(NR):
                        g = mo * NR + r
                        if g >= 2:
                            sync.wait_ge(v_ev, g - 1)
                        sync.dma_start(
                            out=xrt[g % 2][:, :],
                            in_=xr_e[r * 128:(r + 1) * 128, mo * 512:(mo + 1) * 512],
                        ).then_inc(d_xr[g % 2], 16)
                        if g >= 1:
                            gp = g - 1
                            mp, rp = gp // NR, gp % NR
                            sync.wait_ge(v_ev, gp + 1)
                            sync.dma_start(
                                out=xres[rp * 128:(rp + 1) * 128, mp * 512:(mp + 1) * 512],
                                in_=xo[gp % 2][:, :],
                            ).then_inc(d_xo[gp % 2], 16)
                gp = NG - 1
                sync.wait_ge(v_ev, NG)
                sync.dma_start(
                    out=xres[(gp % NR) * 128:((gp % NR) + 1) * 128,
                             (gp // NR) * 512:((gp // NR) + 1) * 512],
                    in_=xo[gp % 2][:, :],
                ).then_inc(d_xo[gp % 2], 16)
                sync.wait_ge(d_xo[0], 16 * (NG // 2))
                sync.wait_ge(d_xo[1], 16 * (NG // 2))

            @blk.scalar
            def _(s_):
                s_.wait_ge(d_ag, 20)
                s_.copy(bass.AP(agh, 0, [[16 * ROWS, 128], [1, 16 * ROWS]]),
                        bass.AP(ag_all, 0, [[16 * ROWS, 128], [1, 16 * ROWS]])).then_inc(f_spl, 1)

            @blk.vector
            def _(v):
                v.wait_ge(f_spl, 1)
                v.tensor_tensor(bass.AP(agl, 0, [[16 * ROWS, 128], [1, 16 * ROWS]]),
                                bass.AP(ag_all, 0, [[16 * ROWS, 128], [1, 16 * ROWS]]),
                                bass.AP(agh, 0, [[16 * ROWS, 128], [1, 16 * ROWS]]),
                                op=ALU.subtract).then_inc(f_spl, 1)
                for mo in range(4):
                    for r in range(NR):
                        g = mo * NR + r
                        v.wait_ge(t_mm, g + 1)
                        v.wait_ge(d_xr[g % 2], 16 * (g // 2 + 1))
                        if g >= 2:
                            v.wait_ge(d_xo[g % 2], 16 * (g // 2))
                        v.tensor_tensor(xo[g % 2][:, :], pf[g % 2][:, :],
                                        xrt[g % 2][:, :], op=ALU.add).then_inc(v_ev, 1)

            @blk.tensor
            def _(t):
                t.wait_ge(f_spl, 2)
                for mo in range(4):
                    t.wait_ge(d_wo[mo % 2], 32 * (mo // 2 + 1))
                    for r in range(NR):
                        g = mo * NR + r
                        if g >= 2:
                            t.wait_ge(v_ev, g - 1)
                        mm = None
                        n_mm = 0
                        for c in range(16):
                            for xs_, ws_ in ((agh, wo_h[mo % 2]), (agl, wo_h[mo % 2]),
                                             (agh, wo_l[mo % 2])):
                                mm = t.matmul(pf[g % 2][:, :],
                                              xs_[:, c, r * 128:(r + 1) * 128],
                                              ws_[:, c, :],
                                              start=(n_mm == 0), stop=(n_mm == 47))
                                n_mm += 1
                        mm.then_inc(t_mm, 1)


def _mk_ffn_sems(nc, g):
    sem = lambda n_: g.enter_context(nc.semaphore("ffn_" + n_))
    names = ["dw", "dxg0", "dxg1", "dxo0", "dxo1", "gln", "gact", "gtp", "getp",
             "gh", "ghv", "gnm", "ge8", "geh", "ga", "gup", "gx", "gcast"]
    return {n_: sem(n_) for n_ in names}


def _stage_ffn(nc, sems, ci, xres, wsplit, ident):
    wf_h, wf_l, wu_h, wu_l = wsplit
    pfx = "gj"[ci]
    with ExitStack() as st:
        sb = lambda n_, s_: st.enter_context(nc.sbuf_tensor(pfx + n_, s_, F32))
        ps = lambda n_, s_: st.enter_context(nc.psum_tensor(pfx + n_, s_, F32))

        wffn_h = st.enter_context(nc.sbuf_tensor(pfx + "wffnh", [128, 16, 136], BF16))
        wffn_l = st.enter_context(nc.sbuf_tensor(pfx + "wffnl", [128, 16, 136], BF16))
        wup_h = st.enter_context(nc.sbuf_tensor(pfx + "wuph", [128, D], BF16))
        wup_l = st.enter_context(nc.sbuf_tensor(pfx + "wupl", [128, D], BF16))
        xg = [sb(f"xg{i}", [128, D]) for i in range(2)]
        lnt = [sb(f"lnt{i}", [128, D]) for i in range(2)]
        lnf_h = [st.enter_context(nc.sbuf_tensor(pfx + f"lnfh{i}", [128, 16, 128], BF16))
                 for i in range(2)]
        lnf_l = [st.enter_context(nc.sbuf_tensor(pfx + f"lnfl{i}", [128, 16, 128], BF16))
                 for i in range(2)]
        bnst = sb("bnst", [128, 24])
        mv = sb("mv", [128, 2])
        rstd = sb("rstd", [128, 1])
        hv = [sb(f"hv{i}", [128, 136]) for i in range(2)]
        gsc = sb("gsc", [128, 8])
        w8 = sb("w8", [128, 8])
        msk8 = sb("msk8", [128, 8])
        gate = sb("gate", [128, 8])
        eh = sb("eh", [128, 128])
        aval = [sb(f"aval{i}", [128, 128]) for i in range(2)]
        atf_h = [st.enter_context(nc.sbuf_tensor(f"{pfx}atfh{i}", [128, 128], BF16))
                 for i in range(2)]
        atf_l = [st.enter_context(nc.sbuf_tensor(f"{pfx}atfl{i}", [128, 128], BF16))
                 for i in range(2)]
        ph = [ps(f"ph{i}", [128, 136]) for i in range(2)]
        ptp = [ps(f"ptp{i}", [128, 128]) for i in range(2)]
        pu = [ps(f"pu{i}", [128, 512]) for i in range(2)]

        d_w = sems["dw"]
        d_xg = [sems["dxg0"], sems["dxg1"]]
        d_xo = [sems["dxo0"], sems["dxo1"]]
        g_ln = sems["gln"]; g_act = sems["gact"]; g_tp = sems["gtp"]; g_etp = sems["getp"]
        g_h = sems["gh"]; g_hv = sems["ghv"]; g_nm = sems["gnm"]; g_e8 = sems["ge8"]
        g_eh = sems["geh"]; g_a = sems["ga"]; g_up = sems["gup"]; g_x = sems["gx"]
        g_cast = sems["gcast"]
        B_LN = ci * NR; B_ACT = ci * NR; B_TP = ci * 17 * NR; B_ETP = ci * 17 * NR
        B_H = ci * NR; B_HV = ci * NR; B_NM = ci * NR; B_E8 = ci * NR
        B_EH = ci * NR; B_A = ci * NR; B_UP = ci * 4 * NR; B_X = ci * 4 * NR
        B_DW = ci * 64; B_DX = ci * 16 * (NR // 2); B_DO = ci * 16 * (NR // 2)
        B_CAST = ci * 17 * NR

        with nc.Block() as blk:
            @blk.sync
            def _(sync):
                for dst_, src_ in ((wffn_h, wf_h), (wffn_l, wf_l)):
                    sync.dma_start(
                        out=dst_[:, :, :],
                        in_=bass.AP(src_, 0, [[136, 128], [136 * 128, 16], [1, 136]]),
                    ).then_inc(d_w, 16)
                sync.dma_start(out=wup_h[:, :], in_=wu_h[:, :]).then_inc(d_w, 16)
                sync.dma_start(out=wup_l[:, :], in_=wu_l[:, :]).then_inc(d_w, 16)
                for r in range(NR):
                    if r >= 2:
                        sync.wait_ge(d_xo[r % 2], B_DO + 16 * (r // 2))
                    sync.dma_start(out=xg[r % 2][:, :],
                                   in_=xres[r * 128:(r + 1) * 128, :]).then_inc(d_xg[r % 2], 16)
                    if r >= 1:
                        sync.wait_ge(g_x, B_X + 4 * r)
                        sync.dma_start(out=xres[(r - 1) * 128:r * 128, :],
                                       in_=xg[(r - 1) % 2][:, :]).then_inc(d_xo[(r - 1) % 2], 16)
                sync.wait_ge(g_x, B_X + 4 * NR)
                sync.dma_start(out=xres[(NR - 1) * 128:NR * 128, :],
                               in_=xg[(NR - 1) % 2][:, :]).then_inc(d_xo[(NR - 1) % 2], 16)
                sync.wait_ge(d_xo[0], B_DO + 16 * (NR // 2))
                sync.wait_ge(d_xo[1], B_DO + 16 * (NR // 2))

            @blk.scalar
            def _(s_):
                for r in range(NR):
                    s_.wait_ge(g_ln, B_LN + r + 1)
                    s_.activation(rstd[:, :], rstd[:, :], AF.Ln)
                    s_.drain()
                    s_.activation(rstd[:, :], rstd[:, :], AF.Exp, scale=-0.5).then_inc(g_act, 1)
                    for c in range(16):
                        gt = 17 * r + c
                        s_.wait_ge(g_tp, B_TP + gt + 1)
                        s_.copy(lnf_h[r % 2][:, c, :], ptp[gt % 2][:, :]).then_inc(g_cast, 1)
                    s_.wait_ge(g_nm, B_NM + r + 1)
                    s_.activation(w8[:, :], hv[r % 2][:, 0:8], AF.Exp, bias=gsc[:, 1:2],
                                  accum_out=gsc[:, 2:3]).then_inc(g_e8, 1)
                    s_.activation(eh[:, :], hv[r % 2][:, 8:136], AF.Exp,
                                  scale=-1.0).then_inc(g_eh, 1)
                    gt = 17 * r + 16
                    s_.wait_ge(g_tp, B_TP + gt + 1)
                    s_.copy(atf_h[r % 2][:, :], ptp[gt % 2][:, :]).then_inc(g_cast, 1)

            @blk.vector
            def _(v):
                for r in range(NR):
                    v.wait_ge(d_xg[r % 2], B_DX + 16 * (r // 2 + 1))
                    for c in range(4):
                        v.bn_stats(bnst[:, 6 * c:6 * (c + 1)], xg[r % 2][:, 512 * c:512 * (c + 1)])
                    v.drain()
                    v.bn_aggr(mv[:, :], bnst[:, :])
                    v.drain()
                    v.tensor_scalar(rstd[:, :], mv[:, 1:2], EPS, None,
                                    op0=ALU.add).then_inc(g_ln, 1)
                    v.wait_ge(g_act, B_ACT + r + 1)
                    if r >= 2:
                        v.wait_ge(g_hv, B_HV + r - 1)
                    v.tensor_scalar(lnt[r % 2][:, :], xg[r % 2][:, :], mv[:, 0:1], rstd[:, 0:1],
                                    op0=ALU.subtract, op1=ALU.mult).then_inc(g_h, 1)
                    for c in range(16):
                        gt = 17 * r + c
                        v.wait_ge(g_cast, B_CAST + 17 * r + c + 1)
                        v.tensor_tensor(lnf_l[r % 2][:, c, :], ptp[gt % 2][:, :],
                                        lnf_h[r % 2][:, c, :],
                                        op=ALU.subtract).then_inc(g_etp, 1)
                    v.wait_ge(g_hv, B_HV + r + 1)
                    v.tensor_copy(hv[r % 2][:, :], ph[r % 2][:, :])
                    v.drain()
                    v.reduce_max(gsc[:, 0:1], hv[r % 2][:, 0:8], axis=AX.X)
                    v.drain()
                    v.tensor_scalar(gsc[:, 1:2], gsc[:, 0:1], -1.0, None,
                                    op0=ALU.mult).then_inc(g_nm, 1)
                    v.wait_ge(g_e8, B_E8 + r + 1)
                    v.reciprocal(gsc[:, 3:4], gsc[:, 2:3])
                    v.drain()
                    v.tensor_scalar(w8[:, :], w8[:, :], gsc[:, 3:4], None, op0=ALU.mult)
                    v.drain()
                    v.reduce_max(gsc[:, 4:5], w8[:, :], axis=AX.X)
                    v.drain()
                    v.tensor_scalar(msk8[:, :], w8[:, :], gsc[:, 4:5], None, op0=ALU.is_ge)
                    v.drain()
                    v.scalar_tensor_tensor(msk8[:, :], msk8[:, :], -10.0, w8[:, :],
                                           op0=ALU.mult, op1=ALU.add)
                    v.drain()
                    v.reduce_max(gsc[:, 5:6], msk8[:, :], axis=AX.X)
                    v.drain()
                    v.tensor_scalar(msk8[:, :], w8[:, :], gsc[:, 5:6], None, op0=ALU.is_ge)
                    v.drain()
                    v.tensor_tensor(gate[:, :], w8[:, :], msk8[:, :], op=ALU.mult)
                    v.wait_ge(g_eh, B_EH + r + 1)
                    v.tensor_scalar(eh[:, :], eh[:, :], 1.0, None, op0=ALU.add)
                    v.drain()
                    v.reciprocal(eh[:, :], eh[:, :])
                    v.drain()
                    if r >= 2:
                        v.wait_ge(g_tp, B_TP + 17 * (r - 1) + 17)
                    v.tensor_tensor(aval[r % 2][:, :], hv[r % 2][:, 8:136], eh[:, :],
                                    op=ALU.mult)
                    v.drain()
                    for e in range(8):
                        v.tensor_scalar(aval[r % 2][:, e * 16:(e + 1) * 16],
                                        aval[r % 2][:, e * 16:(e + 1) * 16],
                                        gate[:, e:e + 1], None, op0=ALU.mult)
                    v.drain()
                    v.nop().then_inc(g_a, 1)
                    gt = 17 * r + 16
                    v.wait_ge(g_cast, B_CAST + 17 * r + 17)
                    v.tensor_tensor(atf_l[r % 2][:, :], ptp[gt % 2][:, :],
                                    atf_h[r % 2][:, :], op=ALU.subtract).then_inc(g_etp, 1)
                    for sp in range(4):
                        v.wait_ge(g_up, B_UP + 4 * r + sp + 1)
                        v.tensor_tensor(xg[r % 2][:, sp * 512:(sp + 1) * 512],
                                        xg[r % 2][:, sp * 512:(sp + 1) * 512],
                                        pu[sp % 2][:, :], op=ALU.add).then_inc(g_x, 1)

            @blk.tensor
            def _(t):
                t.wait_ge(d_w, B_DW + 64)
                for r in range(NR):
                    t.wait_ge(g_h, B_H + r + 1)
                    for c in range(16):
                        gt = 17 * r + c
                        if gt >= 2:
                            t.wait_ge(g_etp, B_ETP + gt - 1)
                        t.transpose(ptp[gt % 2][:, :], lnt[r % 2][:, c * 128:(c + 1) * 128],
                                    ident[:, :]).then_inc(g_tp, 1)
                    t.wait_ge(g_etp, B_ETP + 17 * r + 16)
                    if r >= 2:
                        t.wait_ge(g_nm, B_NM + r - 1)
                    mm = None
                    n_mm = 0
                    for c in range(16):
                        for xs_, ws_ in ((lnf_h[r % 2], wffn_h), (lnf_l[r % 2], wffn_h),
                                         (lnf_h[r % 2], wffn_l)):
                            mm = t.matmul(ph[r % 2][:, :], xs_[:, c, :], ws_[:, c, :],
                                          start=(n_mm == 0), stop=(n_mm == 47))
                            n_mm += 1
                    mm.then_inc(g_hv, 1)
                    t.wait_ge(g_a, B_A + r + 1)
                    gt = 17 * r + 16
                    if gt >= 2:
                        t.wait_ge(g_etp, B_ETP + gt - 1)
                    t.transpose(ptp[gt % 2][:, :], aval[r % 2][:, :],
                                ident[:, :]).then_inc(g_tp, 1)
                    t.wait_ge(g_etp, B_ETP + gt + 1)
                    for sp in range(4):
                        gg = 4 * r + sp
                        if gg >= 2:
                            t.wait_ge(g_x, B_X + gg - 1)
                        mm = None
                        for xs_, ws_ in ((atf_h[r % 2], wup_h), (atf_l[r % 2], wup_h),
                                         (atf_h[r % 2], wup_l)):
                            mm = t.matmul(pu[sp % 2][:, :], xs_[:, :],
                                          ws_[:, sp * 512:(sp + 1) * 512],
                                          start=(xs_ is atf_h[r % 2] and ws_ is wup_h),
                                          stop=(ws_ is wup_l))
                        mm.then_inc(g_up, 1)


def _stage_h(nc, g, xres, wsplit, bin_e, bout_e, lnw_e, lnb_e, ropeb_e,
             ident, ari, aro):
    wi_h, wi_l, wb_h, wb_l = wsplit
    with ExitStack() as st:
        sb = lambda n_, s_: st.enter_context(nc.sbuf_tensor("h" + n_, s_, F32))
        ps = lambda n_, s_: st.enter_context(nc.psum_tensor("h" + n_, s_, F32))

        lnw_r = sb("lnwr", [128, D])
        lnb_r = sb("lnbr", [128, D])
        binr = sb("binr", [128, DFF])
        boutr = sb("boutr", [128, D])
        arep = sb("arep", [128, D])
        onec = sb("onec", [128, 1])
        oner = sb("oner", [1, 128])
        wt_h = [st.enter_context(nc.sbuf_tensor(f"hwth{i}", [128, 16, 512], BF16))
                for i in range(2)]
        wt_l = [st.enter_context(nc.sbuf_tensor(f"hwtl{i}", [128, 16, 512], BF16))
                for i in range(2)]
        xcur = [sb(f"xc{i}", [128, D]) for i in range(2)]
        lnt = sb("lnt", [128, D])
        ht = sb("ht", [128, D])
        hf_h = st.enter_context(nc.sbuf_tensor("hhfh", [128, 16, 128], BF16))
        hf_l = st.enter_context(nc.sbuf_tensor("hhfl", [128, 16, 128], BF16))
        bnst = sb("bnst", [128, 24])
        mv = sb("mv", [128, 2])
        rstd = sb("rstd", [128, 1])
        kt4 = sb("kt4", [128, 4, 512])
        vt2 = sb("vt2", [128, 512])
        sqd = sb("sqd", [128, 512])
        psb = [sb(f"psb{i}", [128, 512]) for i in range(2)]
        rc2 = sb("rc2", [128, 1024])
        rlo = sb("rlo", [128, 1024])
        rtmp = sb("rtmp", [128, 512])
        rtmp2 = sb("rtmp2", [128, 512])
        acc4 = sb("acc4", [128, 4])
        ks = sb("ks", [128, 1])
        yf_h = st.enter_context(nc.sbuf_tensor("hyfh", [128, 16, 128], BF16))
        yf_l = st.enter_context(nc.sbuf_tensor("hyfl", [128, 16, 128], BF16))
        pp = [ps(f"pp{i}", [128, 512]) for i in range(2)]
        ptp = [ps(f"ptp{i}", [128, 128]) for i in range(2)]
        pa = [ps(f"pa{i}", [128, 512]) for i in range(4)]

        h0d = g.enter_context(nc.semaphore("h0d"))
        h0v = g.enter_context(nc.semaphore("h0v"))
        h0m = g.enter_context(nc.semaphore("h0m"))
        with nc.Block() as blk:
            @blk.sync
            def _(sync):
                sync.dma_start(out=binr[0:1, :], in_=bin_e[:, :]).then_inc(h0d, 16)
                sync.dma_start(out=boutr[0:1, :], in_=bout_e[:, :]).then_inc(h0d, 16)
                sync.dma_start(out=lnw_r[0:1, :], in_=lnw_e[:, :]).then_inc(h0d, 16)
                sync.dma_start(out=lnb_r[0:1, :], in_=lnb_e[:, :]).then_inc(h0d, 16)

            @blk.vector
            def _(v):
                v.memset(onec[:, :], 1.0)
                v.memset(oner[:, :], 1.0)
                v.drain()
                v.nop().then_inc(h0v, 1)
                for i in range(24):
                    v.wait_ge(h0m, i + 1)
                    if i < 12:
                        dst = binr[:, i * 512:(i + 1) * 512]
                    elif i < 16:
                        dst = boutr[:, (i - 12) * 512:(i - 11) * 512]
                    elif i < 20:
                        dst = lnw_r[:, (i - 16) * 512:(i - 15) * 512]
                    else:
                        dst = lnb_r[:, (i - 20) * 512:(i - 19) * 512]
                    v.tensor_copy(dst, pa[i % 4][:, :]).then_inc(h0v, 1)

            @blk.tensor
            def _(t):
                t.wait_ge(h0d, 64)
                t.wait_ge(h0v, 1)
                for i in range(24):
                    if i >= 4:
                        t.wait_ge(h0v, i - 2)
                    if i < 12:
                        srow = binr[0:1, i * 512:(i + 1) * 512]
                    elif i < 16:
                        srow = boutr[0:1, (i - 12) * 512:(i - 11) * 512]
                    elif i < 20:
                        srow = lnw_r[0:1, (i - 16) * 512:(i - 15) * 512]
                    else:
                        srow = lnb_r[0:1, (i - 20) * 512:(i - 19) * 512]
                    t.matmul(pa[i % 4][:, :], oner[:, :], srow,
                             start=True, stop=True).then_inc(h0m, 1)

        with ExitStack() as st1:
            sm = lambda n_: g.enter_context(nc.semaphore(n_))
            blk = st1.enter_context(nc.Block())
            d_w = [sm("h1dw0"), sm("h1dw1")]
            d_x = [sm("h1dx0"), sm("h1dx1")]
            d_r = [sm("h1dr0")]
            h_ln = sm("h1ln"); h_act = sm("h1act"); h_ht = sm("h1ht")
            h_tp = sm("h1tp"); h_etp = sm("h1etp"); h_mm = sm("h1mm")
            h_ppc = sm("h1ppc"); h_rope = sm("h1rope"); h_sq = sm("h1sq")
            h_ks = sm("h1ks"); h_rs = sm("h1rs"); h_kn = sm("h1kn")
            h_pv = sm("h1pv"); h_pa = sm("h1pa"); h_cast = sm("h1cast")
            h1_sems = dict(d_w=d_w, d_x=d_x, d_r=d_r, h_ln=h_ln, h_act=h_act,
                           h_ht=h_ht, h_tp=h_tp, h_etp=h_etp, h_mm=h_mm,
                           h_ppc=h_ppc, h_rope=h_rope, h_cast=h_cast,
                           h_sq=h_sq, h_ks=h_ks, h_kn=h_kn, h_pv=h_pv)

            @blk.sync
            def _(sync):
                for r in range(NR):
                    if r >= 2:
                        sync.wait_ge(h_ht, r - 1)
                    sync.dma_start(out=xcur[r % 2][:, :],
                                   in_=xres[r * 128:(r + 1) * 128, :]).then_inc(d_x[r % 2], 16)
                    if r >= 1:
                        sync.wait_ge(h_rope, r)
                    sync.dma_start(out=rc2[:, :],
                                   in_=ropeb_e[r * 128:(r + 1) * 128, 0:1024]).then_inc(d_r[0], 16)
                    sync.dma_start(out=rlo[:, :],
                                   in_=ropeb_e[r * 128:(r + 1) * 128, 1024:2048]).then_inc(d_r[0], 16)
                    for sp in range(8):
                        w_i = r * 8 + sp
                        if w_i >= 2:
                            sync.wait_ge(h_mm, w_i - 1)
                        for dst_, src_ in ((wt_h, wi_h), (wt_l, wi_l)):
                            sync.dma_start(
                                out=dst_[w_i % 2][:, :, :],
                                in_=bass.AP(src_, (4 + sp) * 512,
                                            [[DFF, 128], [DFF * 128, 16], [1, 512]]),
                            ).then_inc(d_w[w_i % 2], 16)

            @blk.scalar
            def _(s_):
                for r in range(NR):
                    s_.wait_ge(h_ln, r + 1)
                    s_.activation(rstd[:, :], rstd[:, :], AF.Ln)
                    s_.drain()
                    s_.activation(rstd[:, :], rstd[:, :], AF.Exp, scale=-0.5).then_inc(h_act, 1)
                    for c in range(16):
                        gt = 16 * r + c
                        s_.wait_ge(h_tp, gt + 1)
                        s_.copy(hf_h[:, c, :], ptp[gt % 2][:, :]).then_inc(h_cast, 1)
                    s_.wait_ge(h_rope, r + 1)
                    if r >= 1:
                        s_.wait_ge(h_ks, r)
                    for sp in range(4):
                        s_.activation(sqd[:, :], kt4[:, sp, :], AF.Square,
                                      accum_out=acc4[:, sp:sp + 1])
                        s_.drain()
                    s_.nop().then_inc(h_sq, 1)
                    s_.wait_ge(h_ks, r + 1)
                    s_.activation(ks[:, :], ks[:, :], AF.Ln)
                    s_.drain()
                    s_.activation(ks[:, :], ks[:, :], AF.Exp, scale=-0.5).then_inc(h_rs, 1)

            @blk.vector
            def _(v):
                for r in range(NR):
                    v.wait_ge(d_x[r % 2], 16 * (r // 2 + 1))
                    if r >= 1:
                        v.wait_ge(h_mm, 8 * r)
                    for c in range(4):
                        v.bn_stats(bnst[:, 6 * c:6 * (c + 1)],
                                   xcur[r % 2][:, 512 * c:512 * (c + 1)])
                    v.drain()
                    v.bn_aggr(mv[:, :], bnst[:, :])
                    v.drain()
                    v.tensor_scalar(rstd[:, :], mv[:, 1:2], EPS, None,
                                    op0=ALU.add).then_inc(h_ln, 1)
                    v.wait_ge(h_act, r + 1)
                    v.tensor_scalar(lnt[:, :], xcur[r % 2][:, :], mv[:, 0:1], rstd[:, 0:1],
                                    op0=ALU.subtract, op1=ALU.mult)
                    v.drain()
                    v.tensor_tensor(ht[:, :], lnt[:, :], lnw_r[:, :], op=ALU.mult)
                    v.drain()
                    v.tensor_tensor(ht[:, :], ht[:, :], lnb_r[:, :], op=ALU.add).then_inc(h_ht, 1)
                    for c in range(16):
                        gt = 16 * r + c
                        v.wait_ge(h_cast, 16 * r + c + 1)
                        v.tensor_tensor(hf_l[:, c, :], ptp[gt % 2][:, :], hf_h[:, c, :],
                                        op=ALU.subtract).then_inc(h_etp, 1)
                    for sp in range(4):
                        v.wait_ge(h_mm, 8 * r + sp + 1)
                        v.tensor_tensor(kt4[:, sp, :], pp[sp % 2][:, :],
                                        binr[:, 2048 + sp * 512:2048 + (sp + 1) * 512],
                                        op=ALU.add).then_inc(h_ppc, 1)
                    v.wait_ge(d_r[0], 32 * (r + 1))
                    v.tensor_tensor(rc2[:, :], rc2[:, :], rlo[:, :], op=ALU.add)
                    v.drain()
                    kev = bass.AP(kt4, 0, [[2048, 128], [2, 512]])
                    kod = bass.AP(kt4, 1, [[2048, 128], [2, 512]])
                    cos, sin = rc2[:, 0:512], rc2[:, 512:1024]
                    v.tensor_tensor(rtmp[:, :], kod, sin, op=ALU.mult)
                    v.tensor_tensor(rtmp2[:, :], kev, sin, op=ALU.mult)
                    v.drain()
                    v.tensor_tensor(kev, kev, cos, op=ALU.mult)
                    v.tensor_tensor(kod, kod, cos, op=ALU.mult)
                    v.drain()
                    v.tensor_tensor(kev, kev, rtmp[:, :], op=ALU.subtract)
                    v.tensor_tensor(kod, kod, rtmp2[:, :], op=ALU.add)
                    v.drain()
                    v.nop().then_inc(h_rope, 1)
                    v.wait_ge(h_sq, r + 1)
                    v.reduce_sum(ks[:, :], acc4[:, :], axis=AX.X).then_inc(h_ks, 1)
                    v.wait_ge(h_rs, r + 1)
                    v.tensor_scalar(bass.AP(kt4, 0, [[2048, 128], [1, 2048]]),
                                    bass.AP(kt4, 0, [[2048, 128], [1, 2048]]),
                                    ks[:, 0:1], None, op0=ALU.mult)
                    v.drain()
                    v.nop().then_inc(h_kn, 1)
                    for sv in range(4):
                        j = 4 * r + sv
                        v.wait_ge(h_mm, 8 * r + 4 + sv + 1)
                        v.tensor_tensor(vt2[:, :], pp[sv % 2][:, :],
                                        binr[:, 4096 + sv * 512:4096 + (sv + 1) * 512],
                                        op=ALU.add)
                        v.drain()
                        if j >= 2:
                            v.wait_ge(h_pa, j - 1)
                        v.tensor_tensor(psb[j % 2][:, :], vt2[:, :], kt4[:, sv, :],
                                        op=ALU.mult).then_inc(h_pv, 1)
                        v.drain()
                        v.nop().then_inc(h_ppc, 1)

            @blk.tensor
            def _(t):
                for r in range(NR):
                    t.wait_ge(h_ht, r + 1)
                    for c in range(16):
                        gt = 16 * r + c
                        if gt >= 2:
                            t.wait_ge(h_etp, gt - 1)
                        t.transpose(ptp[gt % 2][:, :], ht[:, c * 128:(c + 1) * 128],
                                    ident[:, :]).then_inc(h_tp, 1)
                    t.wait_ge(h_etp, 16 * (r + 1))
                    for sp in range(8):
                        w_i = r * 8 + sp
                        g = w_i
                        t.wait_ge(d_w[w_i % 2], 32 * (w_i // 2 + 1))
                        if g >= 2:
                            t.wait_ge(h_ppc, g - 1)
                        if sp >= 4:
                            t.wait_ge(h_kn, r + 1)
                        mm = None
                        n_mm = 0
                        for c in range(16):
                            for xs_, ws_ in ((hf_h, wt_h[w_i % 2]), (hf_l, wt_h[w_i % 2]),
                                             (hf_h, wt_l[w_i % 2])):
                                mm = t.matmul(pp[sp % 2][:, :], xs_[:, c, :], ws_[:, c, :],
                                              start=(n_mm == 0), stop=(n_mm == 47))
                                n_mm += 1
                        mm.then_inc(h_mm, 1)
                        if sp >= 4:
                            sv = sp - 4
                            j = 4 * r + sv
                            t.wait_ge(h_pv, j + 1)
                            t.matmul(pa[sv][0:1, :], onec[:, :], psb[j % 2][:, :],
                                     start=(r == 0), stop=(r == NR - 1)).then_inc(h_pa, 1)

        h2s = h0v; B2S = 25
        h2d = h0d; B2D = 64
        h2cc = h0m; B2CC = 24
        h2m = h1_sems["h_sq"]; B2M = NR
        with nc.Block() as blk:
            @blk.vector
            def _(v):
                for sv in range(4):
                    v.tensor_copy(arep[0:1, sv * 512:(sv + 1) * 512],
                                  pa[sv][0:1, :]).then_inc(h2s, 1)
                for sv in range(4):
                    v.wait_ge(h2m, B2M + sv + 1)
                    v.tensor_copy(arep[:, sv * 512:(sv + 1) * 512], pa[sv][:, :])

            @blk.sync
            def _(sync):
                sync.wait_ge(h2s, B2S + 4)
                sync.dma_start(out=ari[:, :], in_=arep[0:1, :]).then_inc(h2d, 16)

            @blk.gpsimd
            def _(gp):
                gp.wait_ge(h2d, B2D + 16)
                gp.collective_compute(
                    "AllReduce", ALU.add,
                    replica_groups=[[0, 1, 2, 3], [4, 5, 6, 7]],
                    ins=[ari.ap().opt()], outs=[aro.ap().opt()],
                ).then_inc(h2cc, 1)
                gp.wait_ge(h2cc, B2CC + 1)

            @blk.sync
            def _(sync):
                sync.wait_ge(h2cc, B2CC + 1)
                sync.dma_start(out=arep[0:1, :], in_=aro[:, :]).then_inc(h2d, 16)

            @blk.tensor
            def _(t):
                t.wait_ge(h2d, B2D + 32)
                for sv in range(4):
                    t.matmul(pa[sv][:, :], oner[:, :], arep[0:1, sv * 512:(sv + 1) * 512],
                             start=True, stop=True).then_inc(h2m, 1)

        with ExitStack() as st3:
            sm = lambda n_: g.enter_context(nc.semaphore(n_))
            blk = st3.enter_context(nc.Block())
            d_w = h1_sems["d_w"]; d_x = h1_sems["d_x"]; d_r = h1_sems["d_r"]
            d_o = [h1_sems["h_ks"], h1_sems["h_kn"]]
            BDO = [NR, NR]
            h_ln = h1_sems["h_ln"]; h_act = h1_sems["h_act"]; h_ht = h1_sems["h_ht"]
            h_tp = h1_sems["h_tp"]; h_etp = h1_sems["h_etp"]; h_mm = h1_sems["h_mm"]
            h_ppc = h1_sems["h_ppc"]; h_rope = h1_sems["h_rope"]
            h_cast = h1_sems["h_cast"]
            h_y = h1_sems["h_sq"]; BY = NR + 4
            h_x = h1_sems["h_pv"]; BX = 4 * NR
            BLN = NR; BACT = NR; BHT = NR; BTP = 16 * NR; BETP = 16 * NR
            BMM = 8 * NR; BPPC = 8 * NR; BROPE = NR; BCAST = 16 * NR
            BDW = [32 * 32, 32 * 32]; BDX = [16 * (NR // 2), 16 * (NR // 2)]
            BDR = 16 * 2 * NR

            @blk.sync
            def _(sync):
                for r in range(NR):
                    if r >= 2:
                        sync.wait_ge(d_o[r % 2], BDO[r % 2] + 16 * (r // 2))
                    sync.dma_start(out=xcur[r % 2][:, :],
                                   in_=xres[r * 128:(r + 1) * 128, :]).then_inc(d_x[r % 2], 16)
                    if r >= 1:
                        sync.wait_ge(h_rope, BROPE + r)
                    sync.dma_start(out=rc2[:, :],
                                   in_=ropeb_e[r * 128:(r + 1) * 128, 0:1024]).then_inc(d_r[0], 16)
                    sync.dma_start(out=rlo[:, :],
                                   in_=ropeb_e[r * 128:(r + 1) * 128, 1024:2048]).then_inc(d_r[0], 16)
                    for sp in range(8):
                        w_i = r * 8 + sp
                        if w_i >= 2:
                            sync.wait_ge(h_mm, BMM + w_i - 1)
                        if sp < 4:
                            aph = bass.AP(wi_h, sp * 512, [[DFF, 128], [DFF * 128, 16], [1, 512]])
                            apl = bass.AP(wi_l, sp * 512, [[DFF, 128], [DFF * 128, 16], [1, 512]])
                        else:
                            aph = bass.AP(wb_h, (sp - 4) * 512, [[D, 128], [D * 128, 16], [1, 512]])
                            apl = bass.AP(wb_l, (sp - 4) * 512, [[D, 128], [D * 128, 16], [1, 512]])
                        sync.dma_start(out=wt_h[w_i % 2][:, :, :], in_=aph).then_inc(d_w[w_i % 2], 16)
                        sync.dma_start(out=wt_l[w_i % 2][:, :, :], in_=apl).then_inc(d_w[w_i % 2], 16)
                    if r >= 1:
                        sync.wait_ge(h_x, BX + 4 * r)
                        sync.dma_start(out=xres[(r - 1) * 128:r * 128, :],
                                       in_=xcur[(r - 1) % 2][:, :]).then_inc(d_o[(r - 1) % 2], 16)
                sync.wait_ge(h_x, BX + 4 * NR)
                sync.dma_start(out=xres[(NR - 1) * 128:NR * 128, :],
                               in_=xcur[(NR - 1) % 2][:, :]).then_inc(d_o[(NR - 1) % 2], 16)
                sync.wait_ge(d_o[0], BDO[0] + 16 * (NR // 2))
                sync.wait_ge(d_o[1], BDO[1] + 16 * (NR // 2))

            @blk.scalar
            def _(s_):
                for r in range(NR):
                    s_.wait_ge(h_ln, BLN + r + 1)
                    s_.activation(rstd[:, :], rstd[:, :], AF.Ln)
                    s_.drain()
                    s_.activation(rstd[:, :], rstd[:, :], AF.Exp, scale=-0.5).then_inc(h_act, 1)
                    for c in range(16):
                        gt = 32 * r + c
                        s_.wait_ge(h_tp, BTP + gt + 1)
                        s_.copy(hf_h[:, c, :], ptp[gt % 2][:, :]).then_inc(h_cast, 1)
                    for c in range(16):
                        gt = 32 * r + 16 + c
                        s_.wait_ge(h_tp, BTP + gt + 1)
                        s_.copy(yf_h[:, c, :], ptp[gt % 2][:, :]).then_inc(h_cast, 1)

            @blk.vector
            def _(v):
                for r in range(NR):
                    v.wait_ge(d_x[r % 2], BDX[r % 2] + 16 * (r // 2 + 1))
                    if r >= 1:
                        v.wait_ge(h_mm, BMM + 8 * r)
                    for c in range(4):
                        v.bn_stats(bnst[:, 6 * c:6 * (c + 1)],
                                   xcur[r % 2][:, 512 * c:512 * (c + 1)])
                    v.drain()
                    v.bn_aggr(mv[:, :], bnst[:, :])
                    v.drain()
                    v.tensor_scalar(rstd[:, :], mv[:, 1:2], EPS, None,
                                    op0=ALU.add).then_inc(h_ln, 1)
                    v.wait_ge(h_act, BACT + r + 1)
                    v.tensor_scalar(lnt[:, :], xcur[r % 2][:, :], mv[:, 0:1], rstd[:, 0:1],
                                    op0=ALU.subtract, op1=ALU.mult)
                    v.drain()
                    v.tensor_tensor(ht[:, :], lnt[:, :], lnw_r[:, :], op=ALU.mult)
                    v.drain()
                    v.tensor_tensor(ht[:, :], ht[:, :], lnb_r[:, :], op=ALU.add).then_inc(h_ht, 1)
                    for c in range(16):
                        gt = 32 * r + c
                        v.wait_ge(h_cast, BCAST + 32 * r + c + 1)
                        v.tensor_tensor(hf_l[:, c, :], ptp[gt % 2][:, :], hf_h[:, c, :],
                                        op=ALU.subtract).then_inc(h_etp, 1)
                    for sp in range(4):
                        v.wait_ge(h_mm, BMM + 8 * r + sp + 1)
                        v.tensor_tensor(kt4[:, sp, :], pp[sp % 2][:, :],
                                        binr[:, sp * 512:(sp + 1) * 512],
                                        op=ALU.add).then_inc(h_ppc, 1)
                    v.wait_ge(d_r[0], BDR + 32 * (r + 1))
                    v.tensor_tensor(rc2[:, :], rc2[:, :], rlo[:, :], op=ALU.add)
                    v.drain()
                    qev = bass.AP(kt4, 0, [[2048, 128], [2, 512]])
                    qod = bass.AP(kt4, 1, [[2048, 128], [2, 512]])
                    cos, sin = rc2[:, 0:512], rc2[:, 512:1024]
                    v.tensor_tensor(rtmp[:, :], qod, sin, op=ALU.mult)
                    v.tensor_tensor(rtmp2[:, :], qev, sin, op=ALU.mult)
                    v.drain()
                    v.tensor_tensor(qev, qev, cos, op=ALU.mult)
                    v.tensor_tensor(qod, qod, cos, op=ALU.mult)
                    v.drain()
                    v.tensor_tensor(qev, qev, rtmp[:, :], op=ALU.subtract)
                    v.tensor_tensor(qod, qod, rtmp2[:, :], op=ALU.add)
                    v.drain()
                    v.nop().then_inc(h_rope, 1)
                    v.tensor_tensor(bass.AP(kt4, 0, [[2048, 128], [1, 2048]]),
                                    bass.AP(kt4, 0, [[2048, 128], [1, 2048]]),
                                    arep[:, :], op=ALU.mult)
                    v.drain()
                    v.nop().then_inc(h_y, 1)
                    for c in range(16):
                        gt = 32 * r + 16 + c
                        v.wait_ge(h_cast, BCAST + 32 * r + 16 + c + 1)
                        v.tensor_tensor(yf_l[:, c, :], ptp[gt % 2][:, :], yf_h[:, c, :],
                                        op=ALU.subtract).then_inc(h_etp, 1)
                    for sp in range(4):
                        v.wait_ge(h_mm, BMM + 8 * r + 4 + sp + 1)
                        v.tensor_tensor(rtmp[:, :], pp[sp % 2][:, :],
                                        boutr[:, sp * 512:(sp + 1) * 512], op=ALU.add)
                        v.drain()
                        v.tensor_tensor(rtmp[:, :], rtmp[:, :],
                                        lnt[:, sp * 512:(sp + 1) * 512], op=ALU.add)
                        v.drain()
                        v.tensor_tensor(xcur[r % 2][:, sp * 512:(sp + 1) * 512],
                                        xcur[r % 2][:, sp * 512:(sp + 1) * 512],
                                        rtmp[:, :], op=ALU.add)
                        v.drain()
                        v.nop().then_inc(h_ppc, 1)
                        v.nop().then_inc(h_x, 1)

            @blk.tensor
            def _(t):
                for r in range(NR):
                    t.wait_ge(h_ht, BHT + r + 1)
                    for c in range(16):
                        gt = 32 * r + c
                        if gt >= 2:
                            t.wait_ge(h_etp, BETP + gt - 1)
                        t.transpose(ptp[gt % 2][:, :], ht[:, c * 128:(c + 1) * 128],
                                    ident[:, :]).then_inc(h_tp, 1)
                    t.wait_ge(h_etp, BETP + 32 * r + 16)
                    for sp in range(8):
                        w_i = r * 8 + sp
                        t.wait_ge(d_w[w_i % 2], BDW[w_i % 2] + 32 * (w_i // 2 + 1))
                        if w_i >= 2:
                            t.wait_ge(h_ppc, BPPC + w_i - 1)
                        if sp == 4:
                            t.wait_ge(h_y, BY + r + 1)
                            for c in range(16):
                                gt = 32 * r + 16 + c
                                if gt >= 2:
                                    t.wait_ge(h_etp, BETP + gt - 1)
                                t.transpose(ptp[gt % 2][:, :],
                                            bass.AP(kt4, c * 128, [[2048, 128], [1, 128]]),
                                            ident[:, :]).then_inc(h_tp, 1)
                            t.wait_ge(h_etp, BETP + 32 * r + 32)
                        sf_h, sf_l = (hf_h, hf_l) if sp < 4 else (yf_h, yf_l)
                        mm = None
                        n_mm = 0
                        for c in range(16):
                            for xs_, ws_ in ((sf_h, wt_h[w_i % 2]), (sf_l, wt_h[w_i % 2]),
                                             (sf_h, wt_l[w_i % 2])):
                                mm = t.matmul(pp[sp % 2][:, :], xs_[:, c, :], ws_[:, c, :],
                                              start=(n_mm == 0), stop=(n_mm == 47))
                                n_mm += 1
                        mm.then_inc(h_mm, 1)


def make_inputs(inputs):
    X = np.ascontiguousarray(np.asarray(inputs["L0_attn_a_query"], dtype=np.float32))
    Wq = np.asarray(inputs["L0_attn_a_Wq"], np.float32)
    Wk = np.asarray(inputs["L0_attn_a_Wk"], np.float32)
    Wv = np.asarray(inputs["L0_attn_a_Wv"], np.float32)
    Wout = np.ascontiguousarray(np.asarray(inputs["L0_attn_a_Wout"], np.float32))
    lam = np.concatenate([np.asarray(inputs[k], np.float32).ravel() for k in
                          ("L0_attn_a_lambda_q1", "L0_attn_a_lambda_k1",
                           "L0_attn_a_lambda_q2", "L0_attn_a_lambda_k2")])[None, :]
    rmsw = np.asarray(inputs["L0_attn_a_rms_weight"], np.float32)[None, :]
    lnw = np.asarray(inputs["L0_attn_b_ln_w"], np.float32)[None, :]
    lnb = np.asarray(inputs["L0_attn_b_ln_b"], np.float32)[None, :]
    Win = np.ascontiguousarray(np.asarray(inputs["L0_attn_b_W_in"], np.float32))
    bin_ = np.asarray(inputs["L0_attn_b_b_in"], np.float32)[None, :]
    Woutb = np.ascontiguousarray(np.asarray(inputs["L0_attn_b_W_out"], np.float32))
    bout = np.asarray(inputs["L0_attn_b_b_out"], np.float32)[None, :]
    router = np.asarray(inputs["L0_ffn_router_w"], np.float32)
    down = np.asarray(inputs["L0_ffn_down_w"], np.float32)
    up = np.asarray(inputs["L0_ffn_up_w"], np.float32)

    wffn = np.concatenate([router.T, down.reshape(128, D).T], axis=1)
    wffn = np.ascontiguousarray(wffn)
    wup = np.ascontiguousarray(up.transpose(0, 2, 1).reshape(128, D))

    tt = np.arange(T, dtype=np.float64)
    inva = 10000.0 ** (-np.arange(0, 64, 2, dtype=np.float64) / 64.0)
    anga = np.outer(tt, inva)
    cos_a = np.tile(np.cos(anga), (1, 8))
    sin_a = np.tile(np.sin(anga), (1, 8))
    ropea = np.empty((T, 1024), np.float32)
    ropea[:, 0:256] = cos_a; ropea[:, 256:512] = sin_a
    ropea[:, 512:768] = (cos_a - ropea[:, 0:256].astype(np.float64)).astype(np.float32)
    ropea[:, 768:1024] = (sin_a - ropea[:, 256:512].astype(np.float64)).astype(np.float32)

    invb = 10000.0 ** (-np.arange(0, 1024, 2, dtype=np.float64) / 1024.0)
    angb = np.outer(tt, invb)
    cos_b, sin_b = np.cos(angb), np.sin(angb)
    ropeb_full = np.empty((T, 2048), np.float32)
    ropeb_full[:, 0:512] = cos_b; ropeb_full[:, 512:1024] = sin_b
    ropeb_full[:, 1024:1536] = (cos_b - ropeb_full[:, 0:512].astype(np.float64)).astype(np.float32)
    ropeb_full[:, 1536:2048] = (sin_b - ropeb_full[:, 512:1024].astype(np.float64)).astype(np.float32)

    nfreq = T // 2 + 1
    xs = np.linspace(0.0, 1.0, nfreq)
    filt = np.exp(-0.5 * (xs / 0.15) ** 2)
    kern = np.fft.irfft(filt, n=T)
    idx = (np.arange(128)[None, :, None] * 0 +
           np.arange(32)[:, None, None] * 128 +
           np.arange(128)[None, None, :] - np.arange(128)[None, :, None]) % T
    ct64 = kern[idx]
    ctiles = np.empty((32, 128, 256), ml_dtypes.bfloat16)
    ctiles[:, :, 0:128] = ct64.astype(ml_dtypes.bfloat16)
    ctiles[:, :, 128:256] = (ct64 - ctiles[:, :, 0:128].astype(np.float64)
                             ).astype(ml_dtypes.bfloat16)

    masks = np.zeros((128, 4 * 512), np.float32)
    tri = np.tril(np.full((128, 128), NEG, np.float32), -1)
    for d in range(2):
        m = np.zeros((128, 256), np.float32)
        m[:, 0:d * 128] = NEG
        m[:, d * 128:(d + 1) * 128] = tri
        masks[:, d * 256:(d + 1) * 256] = m

    ident = np.eye(128, dtype=np.float32)

    in_maps = []
    for c in range(NCORE):
        b, gidx = c // 4, c % 4
        roff = gidx * ROWS
        in_maps.append({
            "x": X[b],
            "wq": np.ascontiguousarray(Wq[:, gidx * CH:(gidx + 1) * CH]),
            "wk": np.ascontiguousarray(Wk[:, gidx * CH:(gidx + 1) * CH]),
            "wv": np.ascontiguousarray(Wv[:, gidx * CH:(gidx + 1) * CH]),
            "wout": Wout, "w_in": Win, "b_in": bin_, "w_out_b": Woutb, "b_out": bout,
            "ln_w": lnw, "ln_b": lnb, "w_ffn_in": wffn, "w_up": wup,
            "rms_w": rmsw, "lam": lam,
            "ropea": ropea,
            "ropeb": np.ascontiguousarray(ropeb_full[roff:roff + ROWS]),
            "ctiles": ctiles, "masks": masks, "ident": ident,
            "xr": np.ascontiguousarray(X[b, roff:roff + ROWS]),
        })
    return in_maps


def kernel(**inputs):
    in_maps = make_inputs(inputs)
    nc = build_nc()
    res = run_bass_kernel_spmd(nc, in_maps, core_ids=list(range(NCORE)))
    out = np.empty((B, T, D), np.float32)
    for c in range(NCORE):
        out[c // 4, (c % 4) * ROWS:(c % 4 + 1) * ROWS, :] = res.results[c]["out"]
    return out


# version 18 (final)
# speedup vs baseline: 1.0281x; 1.0281x over previous
import math
import numpy as np
import ml_dtypes
import concourse.bass as bass
import concourse.mybir as mybir
from concourse.bass_utils import run_bass_kernel_spmd
from contextlib import ExitStack

F32 = mybir.dt.float32
BF16 = mybir.dt.bfloat16
AF = mybir.ActivationFunctionType
ALU = mybir.AluOpType
AX = mybir.AxisListType

B, T, D = 2, 4096, 2048
H = 16
HPC = 4
NCORE = 8
ROWS = 1024
NT = T // 128
NR = ROWS // 128
CH = 512
DFF = 6144
EPS = 1e-5
LAMBDA_INIT = 0.8 - 0.6 * math.exp(-0.3 * 12)
NEG = -1.0e30


def build_nc(stage_limit=99):
    nc = bass.Bass("TRN2", target_bir_lowering=False)

    ei = lambda n, s: nc.dram_tensor(n, s, F32, kind="ExternalInput")
    x_e = ei("x", [T, D])
    wq_e = ei("wq", [D, CH]); wk_e = ei("wk", [D, CH]); wv_e = ei("wv", [D, CH])
    wout_e = ei("wout", [D, D])
    win_e = ei("w_in", [D, DFF]); bin_e = ei("b_in", [1, DFF])
    woutb_e = ei("w_out_b", [D, D]); bout_e = ei("b_out", [1, D])
    lnw_e = ei("ln_w", [1, D]); lnb_e = ei("ln_b", [1, D])
    wffn_e = ei("w_ffn_in", [D, 136]); wup_e = ei("w_up", [128, D])
    rmsw_e = ei("rms_w", [1, 128])
    lam_e = ei("lam", [1, 256])
    ropea_e = ei("ropea", [T, 1024])
    ropeb_e = ei("ropeb", [ROWS, 2048])
    ctil_e = nc.dram_tensor("ctiles", [32, 128, 256], BF16, kind="ExternalInput")
    mask_e = ei("masks", [128, 4 * 512])
    xr_e = ei("xr", [ROWS, D])
    ident_e = ei("ident", [128, 128])
    out_e = nc.dram_tensor("out", [ROWS, D], F32, kind="ExternalOutput")

    qsp = nc.dram_tensor("qsp", [CH, T], F32)
    ksp = nc.dram_tensor("ksp", [CH, T], F32)
    vsp = nc.dram_tensor("vsp", [T, CH], F32)
    agi = nc.dram_tensor("agi", [4, CH, 1024], F32)
    ago = nc.dram_tensor("ago", [4, D, 1024], F32)
    ari = nc.dram_tensor("ari", [1, D], F32)
    aro = nc.dram_tensor("aro", [1, D], F32)
    xres = nc.dram_tensor("xres", [ROWS, D], F32)
    wq_h = nc.dram_tensor("wq_h", [D, CH], BF16); wq_l = nc.dram_tensor("wq_l", [D, CH], BF16)
    wk_h = nc.dram_tensor("wk_h", [D, CH], BF16); wk_l = nc.dram_tensor("wk_l", [D, CH], BF16)
    wv_h = nc.dram_tensor("wv_h", [D, CH], BF16); wv_l = nc.dram_tensor("wv_l", [D, CH], BF16)
    wo_h = nc.dram_tensor("wo_h", [D, D], BF16); wo_l = nc.dram_tensor("wo_l", [D, D], BF16)
    wi_h = nc.dram_tensor("wi_h", [D, DFF], BF16); wi_l = nc.dram_tensor("wi_l", [D, DFF], BF16)
    wb_h = nc.dram_tensor("wb_h", [D, D], BF16); wb_l = nc.dram_tensor("wb_l", [D, D], BF16)
    wf_h = nc.dram_tensor("wf_h", [D, 136], BF16); wf_l = nc.dram_tensor("wf_l", [D, 136], BF16)
    wu_h = nc.dram_tensor("wu_h", [128, D], BF16); wu_l = nc.dram_tensor("wu_l", [128, D], BF16)

    with ExitStack() as g:
        sb = lambda n, s: g.enter_context(nc.sbuf_tensor(n, s, F32))
        ident = sb("identT", [128, 128])
        lamneg = sb("lamneg", [128, 1])
        sc0 = sb("sc0", [1, 4])

        sA = g.enter_context(nc.semaphore("sA"))
        dA = g.enter_context(nc.semaphore("dA"))
        with nc.Block() as blk, nc.psum_tensor("pA", [128, 4], F32) as pA, \
             nc.sbuf_tensor("lamv", [1, 256], F32) as lamv, \
             nc.sbuf_tensor("ones1", [1, 128], F32) as ones1:
            @blk.sync
            def _(sync):
                sync.dma_start(out=ident[:, :], in_=ident_e[:, :]).then_inc(dA, 16)
                sync.dma_start(out=lamv[:, :], in_=lam_e[:, :]).then_inc(dA, 16)

            @blk.vector
            def _(v):
                v.wait_ge(dA, 32)
                v.memset(ones1[:, :], 1.0)
                v.tensor_tensor(lamv[:, 0:64], lamv[:, 0:64], lamv[:, 64:128], op=ALU.mult)
                v.tensor_tensor(lamv[:, 128:192], lamv[:, 128:192], lamv[:, 192:256], op=ALU.mult)
                v.drain()
                v.reduce_sum(sc0[:, 0:1], lamv[:, 0:64], axis=AX.X)
                v.reduce_sum(sc0[:, 1:2], lamv[:, 128:192], axis=AX.X)
                v.drain()
                v.nop().then_inc(sA, 1)

            @blk.scalar
            def _(s):
                s.wait_ge(sA, 1)
                s.activation(sc0[:, 0:2], sc0[:, 0:2], AF.Exp)
                s.drain()
                s.nop().then_inc(sA, 1)

            @blk.vector
            def _(v):
                v.wait_ge(sA, 2)
                v.tensor_tensor(sc0[:, 2:3], sc0[:, 1:2], sc0[:, 0:1], op=ALU.subtract)
                v.drain()
                v.tensor_scalar(sc0[:, 2:3], sc0[:, 2:3], -float(LAMBDA_INIT), None, op0=ALU.add)
                v.drain()
                v.nop().then_inc(sA, 1)

            @blk.tensor
            def _(t):
                t.wait_ge(sA, 3)
                t.matmul(pA[:, 0:1], ones1[:, :], sc0[:, 2:3], start=True, stop=True).then_inc(sA, 1)

            @blk.vector
            def _(v):
                v.wait_ge(sA, 4)
                v.tensor_copy(lamneg[:, :], pA[:, 0:1])

        late_jobs = [
            (wout_e, wo_h, wo_l, D, D), (win_e, wi_h, wi_l, D, DFF),
            (woutb_e, wb_h, wb_l, D, D), (wffn_e, wf_h, wf_l, D, 136),
            (wup_e, wu_h, wu_l, 128, D),
        ]

        phB = ExitStack()
        if stage_limit >= 1:
            wsb_h = [phB.enter_context(nc.sbuf_tensor(f"wsbh{n}", [128, 16, 512], BF16))
                     for n in "qkv"]
            wsb_l = [phB.enter_context(nc.sbuf_tensor(f"wsbl{n}", [128, 16, 512], BF16))
                     for n in "qkv"]
            _stage_split(nc, g, [wq_e, wk_e, wv_e], wsb_h, wsb_l)
            _stage_b(nc, g, x_e, ropea_e, (wsb_h, wsb_l), ident, qsp, ksp, vsp)
        phB.close()

        ph1 = ExitStack()
        if stage_limit >= 2:
            attn_c = ph1.enter_context(nc.sbuf_tensor("attn_c", [128, NT, CH], F32))
            _stage_c(nc, g, qsp, ksp, vsp, mask_e, lamneg, attn_c, late_jobs)

        if stage_limit >= 3:
            e_cc = g.enter_context(nc.semaphore("e_cc"))
            _stage_d(nc, g, attn_c, ctil_e, rmsw_e, ident, agi, ago, e_cc)
        ph1.close()

        if stage_limit >= 5:
            _stage_f(nc, g, ago, (wo_h, wo_l), xr_e, xres, e_cc)

        ffn_sems = None
        if stage_limit >= 6:
            ffn_sems = _mk_ffn_sems(nc, g)
            _stage_ffn(nc, ffn_sems, 0, xres, (wf_h, wf_l, wu_h, wu_l), ident)

        if stage_limit >= 7:
            _stage_h(nc, g, xres, (wi_h, wi_l, wb_h, wb_l), bin_e, bout_e, lnw_e,
                     lnb_e, ropeb_e, ident, ari, aro)

        if stage_limit >= 8:
            _stage_ffn(nc, ffn_sems, 1, xres, (wf_h, wf_l, wu_h, wu_l), ident,
                       out_dram=out_e)

    return nc



def _stage_split(nc, g, srcs, wsb_h, wsb_l):
    tiles = [(se, w, t) for w, se in enumerate(srcs) for t in range(16)]
    with ExitStack() as st:
        f32t = [st.enter_context(nc.sbuf_tensor(f"sp_f{i}", [128, CH], F32))
                for i in range(2)]
        d_in = [g.enter_context(nc.semaphore(f"sp_di{i}")) for i in range(2)]
        s_hi = g.enter_context(nc.semaphore("sp_hi"))
        v_lo = g.enter_context(nc.semaphore("sp_lo"))
        with nc.Block() as blk:
            @blk.sync
            def _(sync):
                for i, (se, w, t) in enumerate(tiles):
                    if i >= 2:
                        sync.wait_ge(v_lo, i - 1)
                    sync.dma_start(out=f32t[i % 2][:, :],
                                   in_=se[t * 128:(t + 1) * 128, :]).then_inc(d_in[i % 2], 16)

            @blk.scalar
            def _(s_):
                for i, (se, w, t) in enumerate(tiles):
                    s_.wait_ge(d_in[i % 2], 16 * (i // 2 + 1))
                    s_.copy(wsb_h[w][:, t, :], f32t[i % 2][:, :]).then_inc(s_hi, 1)

            @blk.vector
            def _(v):
                for i, (se, w, t) in enumerate(tiles):
                    v.wait_ge(s_hi, i + 1)
                    v.tensor_tensor(wsb_l[w][:, t, :], f32t[i % 2][:, :],
                                    wsb_h[w][:, t, :],
                                    op=ALU.subtract).then_inc(v_lo, 1)


def _stage_b(nc, g, x_e, ropea_e, wsplit, ident, qsp, ksp, vsp):
    with ExitStack() as st:
        sb = lambda n, s: st.enter_context(nc.sbuf_tensor(n, s, F32))
        ps = lambda n, s: st.enter_context(nc.psum_tensor(n, s, F32))
        sem = lambda n: g.enter_context(nc.semaphore(n))

        xt = [sb(f"xt{i}", [128, D]) for i in range(2)]
        rt = [sb(f"rt{i}", [128, 1024]) for i in range(2)]
        rc = [sb(f"rc{i}", [128, 512]) for i in range(2)]
        lnt = [sb(f"lnt{i}", [128, D]) for i in range(2)]
        lnf_h = [st.enter_context(nc.sbuf_tensor(f"lnfh{i}", [128, 16, 128], BF16))
                 for i in range(2)]
        lnf_l = [st.enter_context(nc.sbuf_tensor(f"lnfl{i}", [128, 16, 128], BF16))
                 for i in range(2)]
        bnst = sb("bnst", [128, 24])
        mv = sb("mv", [128, 2])
        rstd = sb("rstd", [128, 1])
        rtmp = sb("rtmp", [128, 256])
        wsb_h, wsb_l = wsplit
        qr = [sb(f"qr{i}", [128, 512]) for i in range(2)]
        kr = [sb(f"kr{i}", [128, 512]) for i in range(2)]
        vt = [sb(f"vt{i}", [128, 512]) for i in range(2)]
        qkf = [sb(f"qkf{i}", [128, 1024]) for i in range(2)]
        ptp = [ps(f"ptp{i}", [128, 128]) for i in range(2)]
        pq = [ps(f"pq{i}", [128, 512]) for i in range(2)]
        pk = [ps(f"pk{i}", [128, 512]) for i in range(2)]
        pv = [ps(f"pv{i}", [128, 512]) for i in range(2)]

        b_cast = sem("b_cast")
        d_x = [sem("b_dx0"), sem("b_dx1")]
        d_r = [sem("b_dr0"), sem("b_dr1")]
        d_sp = [sem("b_dsp0"), sem("b_dsp1")]
        b_vst = sem("b_vst"); b_act = sem("b_act"); b_ln = sem("b_ln")
        b_tp = sem("b_tp"); b_etp = sem("b_etp"); b_prj = sem("b_prj")
        b_ev = sem("b_ev"); b_qkf = sem("b_qkf")

        with nc.Block() as blk:
            @blk.sync
            def _(sync):
                for i in range(NT):
                    if i >= 2:
                        sync.wait_ge(b_ln, i - 1)
                        sync.wait_ge(b_ev, i - 1)
                    sync.dma_start(out=xt[i % 2][:, :],
                                   in_=x_e[i * 128:(i + 1) * 128, :]).then_inc(d_x[i % 2], 16)
                    sync.dma_start(out=rt[i % 2][:, :],
                                   in_=ropea_e[i * 128:(i + 1) * 128, :]).then_inc(d_r[i % 2], 16)
                    if i >= 1:
                        _emit_spill(sync, i - 1, qkf, vt, qsp, ksp, vsp, b_qkf, d_sp)
                _emit_spill(sync, NT - 1, qkf, vt, qsp, ksp, vsp, b_qkf, d_sp)
                sync.wait_ge(d_sp[0], 144 * (NT // 2))
                sync.wait_ge(d_sp[1], 144 * (NT // 2))

            @blk.scalar
            def _(s):
                for i in range(NT):
                    s.wait_ge(b_vst, i + 1)
                    s.activation(rstd[:, :], rstd[:, :], AF.Ln)
                    s.drain()
                    s.activation(rstd[:, :], rstd[:, :], AF.Exp, scale=-0.5).then_inc(b_act, 1)
                    for c in range(16):
                        gt = 24 * i + c
                        s.wait_ge(b_tp, gt + 1)
                        s.copy(lnf_h[i % 2][:, c, :], ptp[gt % 2][:, :]).then_inc(b_cast, 1)

            @blk.vector
            def _(v):
                for i in range(NT):
                    bb = i % 2
                    v.wait_ge(d_x[i % 2], 16 * (i // 2 + 1))
                    for c in range(4):
                        v.bn_stats(bnst[:, 6 * c:6 * (c + 1)], xt[bb][:, 512 * c:512 * (c + 1)])
                    v.drain()
                    v.bn_aggr(mv[:, :], bnst[:, :])
                    v.drain()
                    v.tensor_scalar(rstd[:, :], mv[:, 1:2], EPS, None,
                                    op0=ALU.add).then_inc(b_vst, 1)
                    v.wait_ge(b_act, i + 1)
                    v.tensor_scalar(lnt[bb][:, :], xt[bb][:, :], mv[:, 0:1], rstd[:, 0:1],
                                    op0=ALU.subtract, op1=ALU.mult).then_inc(b_ln, 1)
                    for c in range(16):
                        gt = 24 * i + c
                        v.wait_ge(b_cast, 16 * i + c + 1)
                        v.tensor_tensor(lnf_l[bb][:, c, :], ptp[gt % 2][:, :],
                                        lnf_h[bb][:, c, :],
                                        op=ALU.subtract).then_inc(b_etp, 1)
                    v.wait_ge(d_r[i % 2], 16 * (i // 2 + 1))
                    v.tensor_tensor(rc[bb][:, :], rt[bb][:, 0:512], rt[bb][:, 512:1024], op=ALU.add)
                    v.drain()
                    if i >= 2:
                        v.wait_ge(d_sp[i % 2], 144 * (i // 2))
                    for widx, (psrc, dst) in enumerate(((pq, qr), (pk, kr))):
                        v.wait_ge(b_prj, 3 * i + widx + 1)
                        src = psrc[bb]
                        cos, sin = rc[bb][:, 0:256], rc[bb][:, 256:512]
                        ev, od = src[:, 0::2], src[:, 1::2]
                        de, do = dst[bb][:, 0::2], dst[bb][:, 1::2]
                        v.tensor_tensor(de, ev, cos, op=ALU.mult)
                        v.tensor_tensor(do, od, cos, op=ALU.mult)
                        v.tensor_tensor(rtmp[:, :], od, sin, op=ALU.mult)
                        v.drain()
                        v.tensor_tensor(de, de, rtmp[:, :], op=ALU.subtract)
                        v.drain()
                        v.tensor_tensor(rtmp[:, :], ev, sin, op=ALU.mult)
                        v.drain()
                        v.tensor_tensor(do, do, rtmp[:, :], op=ALU.add)
                        v.drain()
                    v.wait_ge(b_prj, 3 * i + 3)
                    v.tensor_copy(vt[bb][:, :], pv[bb][:, :]).then_inc(b_ev, 1)
                    for c in range(8):
                        gt = 24 * i + 16 + c
                        v.wait_ge(b_tp, gt + 1)
                        v.tensor_copy(qkf[bb][:, c * 128:(c + 1) * 128],
                                      ptp[gt % 2][:, :]).then_inc(b_etp, 1)
                    v.drain()
                    v.nop().then_inc(b_qkf, 1)

            @blk.tensor
            def _(t):
                for i in range(NT):
                    bb = i % 2
                    t.wait_ge(b_ln, i + 1)
                    for c in range(16):
                        gt = 24 * i + c
                        if gt >= 2:
                            t.wait_ge(b_etp, gt - 1)
                        t.transpose(ptp[gt % 2][:, :], lnt[bb][:, c * 128:(c + 1) * 128],
                                    ident[:, :]).then_inc(b_tp, 1)
                    t.wait_ge(b_etp, 24 * i + 16)
                    if i >= 2:
                        t.wait_ge(b_ev, i - 1)
                    for wi_, pdst in ((0, pq), (1, pk), (2, pv)):
                        n_mm = 0
                        for c in range(16):
                            for xs_, ws_ in ((lnf_h[bb], wsb_h[wi_]), (lnf_l[bb], wsb_h[wi_]),
                                             (lnf_h[bb], wsb_l[wi_])):
                                mm = t.matmul(pdst[bb][:, :], xs_[:, c, :], ws_[:, c, :],
                                              start=(n_mm == 0), stop=(n_mm == 47))
                                n_mm += 1
                        mm.then_inc(b_prj, 1)
                    t.wait_ge(b_ev, i + 1)
                    for c in range(8):
                        gt = 24 * i + 16 + c
                        if gt >= 2:
                            t.wait_ge(b_etp, gt - 1)
                        src = qr[bb] if c < 4 else kr[bb]
                        t.transpose(ptp[gt % 2][:, :], src[:, (c % 4) * 128:((c % 4) + 1) * 128],
                                    ident[:, :]).then_inc(b_tp, 1)


def _emit_spill(sync, i, qkf, vt, qsp, ksp, vsp, b_qkf, d_sp):
    bb = i % 2
    sync.wait_ge(b_qkf, i + 1)
    for j in range(4):
        sync.dma_start(out=qsp[j * 128:(j + 1) * 128, i * 128:(i + 1) * 128],
                       in_=qkf[bb][:, j * 128:(j + 1) * 128]).then_inc(d_sp[bb], 16)
        sync.dma_start(out=ksp[j * 128:(j + 1) * 128, i * 128:(i + 1) * 128],
                       in_=qkf[bb][:, 512 + j * 128:512 + (j + 1) * 128]).then_inc(d_sp[bb], 16)
    sync.dma_start(out=vsp[i * 128:(i + 1) * 128, :], in_=vt[bb][:, :]).then_inc(d_sp[bb], 16)


def _stage_c(nc, g, qsp, ksp, vsp, mask_e, lamneg, attn_c, late_jobs):
    NSP = 16
    sched = []
    for h in range(HPC):
        for j in range(NSP):
            for s in range(2 * j + 2):
                sched.append((h, j, s))
    idx = {t: n + 1 for n, t in enumerate(sched)}
    span_idx = {}
    n = 0
    for h in range(HPC):
        for j in range(NSP):
            n += 1
            span_idx[(h, j)] = n

    bmask_cnt = {}
    nb = 0
    for (h, j, s) in sched:
        if s >= 2 * j:
            nb += 1
            bmask_cnt[(h, j, s)] = nb

    with ExitStack() as st:
        sb = lambda n_, s_: st.enter_context(nc.sbuf_tensor(n_, s_, F32))
        ps = lambda n_, s_: st.enter_context(nc.psum_tensor(n_, s_, F32))
        sem = lambda n_: g.enter_context(nc.semaphore(n_))

        kt = [sb(f"c_kt{i}", [128, T]) for i in range(2)]
        qt = [sb(f"c_qt{i}", [128, T]) for i in range(2)]
        vsb = [sb(f"c_v{i}", [128, NT, 132]) for i in range(2)]
        e1 = [sb(f"c_e1{i}", [128, 256]) for i in range(2)]
        e2 = [sb(f"c_e2{i}", [128, 256]) for i in range(2)]
        msk = sb("c_msk", [128, 512])
        rz = sb("c_rz", [128, 4])
        tm1 = sb("c_tm1", [128, 128])
        tm2 = sb("c_tm2", [128, 128])
        sc = [[ps(f"c_sc{i}{p}", [128, 256]) for p in range(2)] for i in range(2)]
        av = [ps(f"c_av{i}", [128, 132]) for i in range(4)]

        d_qk = [sem("c_dqk0"), sem("c_dqk1")]
        d_v = [sem("c_dv0"), sem("c_dv1")]
        d_m = sem("c_dm")
        t_sc = sem("c_tsc"); t_av = sem("c_tav")
        a_exp = sem("c_aexp")
        v_msk = sem("c_vmsk"); v_avev = sem("c_avev"); v_ones = sem("c_vones")

        SPC = 2048
        tiles2 = []
        for (se_, he_, le_, rows_, cols_) in late_jobs:
            for r0_ in range(0, rows_, 128):
                for c0_ in range(0, cols_, SPC):
                    tiles2.append((se_, he_, le_, r0_, c0_, min(SPC, cols_ - c0_)))
        f2 = [st.enter_context(nc.sbuf_tensor(f"c_spf{i}", [128, SPC], F32))
              for i in range(2)]
        h2 = [st.enter_context(nc.sbuf_tensor(f"c_sph{i}", [128, SPC], BF16))
              for i in range(2)]
        l2 = [st.enter_context(nc.sbuf_tensor(f"c_spl{i}", [128, SPC], BF16))
              for i in range(2)]
        sd = [sem("c_spd0"), sem("c_spd1")]

        with nc.Block() as blk:
            @blk.gpsimd
            def _(gp):
                nlt = len(tiles2)
                for i in range(min(2, nlt)):
                    se_, he_, le_, r0_, c0_, cw_ = tiles2[i]
                    gp.dma_start(out=f2[i % 2][:, 0:cw_],
                                 in_=se_[r0_:r0_ + 128, c0_:c0_ + cw_]).then_inc(sd[i % 2], 16)
                for i in range(nlt):
                    b = i % 2
                    k = i // 2
                    se_, he_, le_, r0_, c0_, cw_ = tiles2[i]
                    gp.wait_ge(sd[b], 48 * k + 16)
                    gp.tensor_copy(h2[b][:, 0:cw_], f2[b][:, 0:cw_])
                    gp.drain()
                    gp.tensor_tensor(l2[b][:, 0:cw_], f2[b][:, 0:cw_], h2[b][:, 0:cw_],
                                     op=ALU.subtract)
                    gp.drain()
                    gp.dma_start(out=he_[r0_:r0_ + 128, c0_:c0_ + cw_],
                                 in_=h2[b][:, 0:cw_]).then_inc(sd[b], 16)
                    gp.dma_start(out=le_[r0_:r0_ + 128, c0_:c0_ + cw_],
                                 in_=l2[b][:, 0:cw_]).then_inc(sd[b], 16)
                    if i + 2 < nlt:
                        se2, he2, le2, r20, c20, cw2 = tiles2[i + 2]
                        gp.dma_start(out=f2[b][:, 0:cw2],
                                     in_=se2[r20:r20 + 128, c20:c20 + cw2]).then_inc(sd[b], 16)
                gp.wait_ge(sd[0], 48 * ((nlt + 1) // 2))
                gp.wait_ge(sd[1], 48 * (nlt // 2))

            @blk.sync
            def _(sync):
                sync.dma_start(out=msk[:, :], in_=mask_e[:, 0:512]).then_inc(d_m, 16)
                for h in range(HPC):
                    hb = h % 2
                    if h >= 2:
                        sync.wait_ge(t_av, 272 * (h - 1))
                    sync.dma_start(out=kt[hb][:, :],
                                   in_=ksp[h * 128:(h + 1) * 128, :]).then_inc(d_qk[hb], 16)
                    sync.dma_start(out=qt[hb][:, :],
                                   in_=qsp[h * 128:(h + 1) * 128, :]).then_inc(d_qk[hb], 16)
                    sync.dma_start(
                        out=bass.AP(vsb[hb], 0, [[132 * NT, 128], [132, NT], [1, 128]]),
                        in_=bass.AP(vsp, h * 128, [[CH, 128], [CH * 128, NT], [1, 128]]),
                    ).then_inc(d_v[hb], 16)

            @blk.vector
            def _(v):
                v.wait_ge(d_m, 16)
                for h in range(HPC):
                    hb = h % 2
                    if h >= 2:
                        v.wait_ge(t_av, 272 * (h - 1))
                    v.memset(vsb[hb][:, :, 128:132], 0.0)
                    v.drain()
                    v.memset(vsb[hb][:, :, 128:129], 1.0)
                    v.drain()
                    v.nop().then_inc(v_ones, 1)
                    for j in range(NSP):
                        for s in (2 * j, 2 * j + 1):
                            d = s - 2 * j
                            v.wait_ge(t_sc, idx[(h, j, s)])
                            v.tensor_tensor(sc[s % 2][0][:, :], sc[s % 2][0][:, :],
                                            msk[:, d * 256:(d + 1) * 256], op=ALU.add)
                            v.tensor_tensor(sc[s % 2][1][:, :], sc[s % 2][1][:, :],
                                            msk[:, d * 256:(d + 1) * 256],
                                            op=ALU.add).then_inc(v_msk, 1)
                        v.wait_ge(t_av, idx[(h, j, 2 * j + 1)])
                        for qb in range(2):
                            v.reciprocal(rz[:, 0:1], av[qb][:, 128:129])
                            v.reciprocal(rz[:, 1:2], av[2 + qb][:, 128:129])
                            v.drain()
                            v.tensor_scalar(tm1[:, :], av[qb][:, 0:128],
                                            rz[:, 0:1], None, op0=ALU.mult)
                            v.tensor_scalar(tm2[:, :], av[2 + qb][:, 0:128],
                                            rz[:, 1:2], None, op0=ALU.mult)
                            v.drain()
                            v.scalar_tensor_tensor(
                                attn_c[:, 2 * j + qb, h * 128:(h + 1) * 128],
                                tm2[:, :], lamneg[:, 0:1], tm1[:, :],
                                op0=ALU.mult, op1=ALU.add)
                        v.drain()
                        v.nop().then_inc(v_avev, 1)

            @blk.scalar
            def _(s_):
                for (h, j, s) in sched:
                    if s >= 2 * j:
                        s_.wait_ge(v_msk, bmask_cnt[(h, j, s)])
                    else:
                        s_.wait_ge(t_sc, idx[(h, j, s)])
                    if idx[(h, j, s)] > 2:
                        s_.wait_ge(t_av, idx[(h, j, s)] - 2)
                    s_.activation(e1[s % 2][:, :], sc[s % 2][0][:, :], AF.Exp, scale=0.125)
                    s_.activation(e2[s % 2][:, :], sc[s % 2][1][:, :], AF.Exp,
                                  scale=0.125).then_inc(a_exp, 1)

            @blk.tensor
            def _(t):
                for h in range(HPC):
                    hb = h % 2
                    t.wait_ge(d_qk[hb], 32 * (h // 2 + 1))
                    t.wait_ge(d_v[hb], 16 * (h // 2 + 1))
                    t.wait_ge(v_ones, h + 1)
                    for j in range(NSP):
                        L = 2 * j + 2
                        for s in range(L):
                            if idx[(h, j, s)] > 2:
                                t.wait_ge(a_exp, idx[(h, j, s)] - 2)
                            t.matmul(sc[s % 2][0][:, :],
                                     kt[hb][0:64, s * 128:(s + 1) * 128],
                                     qt[hb][0:64, j * 256:(j + 1) * 256],
                                     start=True, stop=True, tile_position=(0, 0))
                            t.matmul(sc[s % 2][1][:, :],
                                     kt[hb][64:128, s * 128:(s + 1) * 128],
                                     qt[hb][64:128, j * 256:(j + 1) * 256],
                                     start=True, stop=True,
                                     tile_position=(64, 0)).then_inc(t_sc, 1)
                            if s >= 1:
                                _emit_av(t, h, j, s - 1, L, e1, e2, vsb, av, idx,
                                         span_idx, a_exp, t_av, v_avev, hb)
                        _emit_av(t, h, j, L - 1, L, e1, e2, vsb, av, idx,
                                 span_idx, a_exp, t_av, v_avev, hb)


def _emit_av(t, h, j, s, L, e1, e2, vsb, av, idx, span_idx, a_exp, t_av, v_avev, hb):
    t.wait_ge(a_exp, idx[(h, j, s)])
    if s == 0 and span_idx[(h, j)] > 1:
        t.wait_ge(v_avev, span_idx[(h, j)] - 1)
    mm = None
    for p, e in ((0, e1), (1, e2)):
        for qb in range(2):
            mm = t.matmul(av[2 * p + qb][:, :],
                          e[s % 2][:, qb * 128:(qb + 1) * 128],
                          vsb[hb][:, s, :],
                          start=(s == 0), stop=(s == L - 1))
    mm.then_inc(t_av, 1)


def _stage_d(nc, g, attn_c, ctil_e, rmsw_e, ident, agi, ago, e_cc):
    with ExitStack() as st:
        sb = lambda n_, s_: st.enter_context(nc.sbuf_tensor(n_, s_, F32))
        ps = lambda n_, s_: st.enter_context(nc.psum_tensor(n_, s_, F32))
        sem = lambda n_: g.enter_context(nc.semaphore(n_))

        ctl_raw = st.enter_context(nc.sbuf_tensor("d_ctraw", [128, NT, 256], BF16))
        ach = st.enter_context(nc.sbuf_tensor("d_ach", [128, NT, 512], BF16))
        acl = st.enter_context(nc.sbuf_tensor("d_acl", [128, NT, 512], BF16))
        d_split = g.enter_context(nc.semaphore("d_split"))
        rmsw_t = sb("d_rmsw", [128, 128])
        rw_row = sb("d_rwrow", [1, 128])
        ones_d = sb("d_ones", [1, 128])
        sqd = sb("d_sqd", [128, 4, 128])
        acc4 = sb("d_acc4", [128, 4])
        rs4 = sb("d_rs4", [128, 4])
        af = [sb(f"d_af{i}", [128, 512]) for i in range(2)]
        agf = [sb(f"d_agf{i}", [128, 512]) for i in range(2)]
        cv = [ps(f"d_cv{i}", [128, 512]) for i in range(2)]
        ptp = [ps(f"d_ptp{i}", [128, 128]) for i in range(2)]
        pbc = ps("d_pbc", [128, 128])

        d_ct = sem("d_dct"); d_rm = d_ct
        d_ag = [sem("d_dag0"), sem("d_dag1")]
        s_pr = sem("d_spr")
        t_cv = sem("d_tcv"); a_sq = sem("d_asq"); v_ts = sem("d_vts")
        a_rs = sem("d_ars"); v_af = sem("d_vaf")
        t_tp = sem("d_ttp"); v_etp = sem("d_vetp"); v_agf = sem("d_vagf")

        with nc.Block() as blk:
            @blk.sync
            def _(sync):
                sync.dma_start(
                    out=bass.AP(ctl_raw, 0, [[NT * 256, 128], [256, NT], [1, 256]]),
                    in_=bass.AP(ctil_e, 0, [[256, 128], [128 * 256, NT], [1, 256]]),
                ).then_inc(d_ct, 16)
                sync.dma_start(out=rw_row[:, :], in_=rmsw_e[:, :]).then_inc(d_rm, 16)
                for i in range(NT):
                    sync.wait_ge(v_agf, i + 1)
                    qq, tl = i // 8, i % 8
                    for u in range(4):
                        sync.dma_start(
                            out=bass.AP(agi, qq * CH * 1024 + u * 128 * 1024 + tl * 128,
                                        [[1024, 128], [1, 128]]),
                            in_=agf[i % 2][:, u * 128:(u + 1) * 128],
                        ).then_inc(d_ag[i % 2], 16)
                sync.wait_ge(d_ag[0], 64 * (NT // 2))
                sync.wait_ge(d_ag[1], 64 * (NT // 2))

            @blk.vector
            def _(v):
                v.memset(ones_d[:, :], 1.0)
                v.wait_ge(d_split, 1)
                v.tensor_tensor(bass.AP(acl, 0, [[NT * 512, 128], [1, NT * 512]]),
                                bass.AP(attn_c, 0, [[NT * CH, 128], [1, NT * CH]]),
                                bass.AP(ach, 0, [[NT * 512, 128], [1, NT * 512]]),
                                op=ALU.subtract).then_inc(d_split, 1)
                v.wait_ge(d_rm, 32)
                v.tensor_scalar(rw_row[:, :], rw_row[:, :], 1.0 - float(LAMBDA_INIT),
                                None, op0=ALU.mult).then_inc(s_pr, 1)
                v.wait_ge(s_pr, 2)
                v.tensor_copy(rmsw_t[:, :], pbc[:, :]).then_inc(s_pr, 1)
                for i in range(NT):
                    v.wait_ge(a_sq, i + 1)
                    if i >= 1:
                        v.wait_ge(a_rs, i)
                    v.tensor_scalar(acc4[:, :], acc4[:, :], 1.0 / 128.0, EPS,
                                    op0=ALU.mult, op1=ALU.add).then_inc(v_ts, 1)
                    v.wait_ge(a_rs, i + 1)
                    if i >= 2:
                        v.wait_ge(t_tp, 4 * (i - 1))
                    for hh in range(4):
                        v.scalar_tensor_tensor(af[i % 2][:, hh * 128:(hh + 1) * 128],
                                               cv[i % 2][:, hh * 128:(hh + 1) * 128],
                                               rs4[:, hh:hh + 1], rmsw_t[:, :],
                                               op0=ALU.mult, op1=ALU.mult)
                    v.drain()
                    v.nop().then_inc(v_af, 1)
                    if i >= 2:
                        v.wait_ge(d_ag[i % 2], 64 * (i // 2))
                    for u in range(4):
                        gt = 4 * i + u
                        v.wait_ge(t_tp, gt + 1)
                        v.tensor_copy(agf[i % 2][:, u * 128:(u + 1) * 128],
                                      ptp[gt % 2][:, :]).then_inc(v_etp, 1)
                    v.drain()
                    v.nop().then_inc(v_agf, 1)

            @blk.scalar
            def _(s_):
                s_.copy(bass.AP(ach, 0, [[NT * 512, 128], [1, NT * 512]]),
                        bass.AP(attn_c, 0, [[NT * CH, 128], [1, NT * CH]])).then_inc(d_split, 1)
                for i in range(NT):
                    s_.wait_ge(t_cv, i + 1)
                    if i >= 1:
                        s_.wait_ge(v_ts, i)
                    for hh in range(4):
                        s_.activation(sqd[:, hh, :], cv[i % 2][:, hh * 128:(hh + 1) * 128],
                                      AF.Square, accum_out=acc4[:, hh:hh + 1])
                    s_.drain()
                    s_.nop().then_inc(a_sq, 1)
                    s_.wait_ge(v_ts, i + 1)
                    s_.activation(rs4[:, :], acc4[:, :], AF.Ln)
                    s_.drain()
                    s_.activation(rs4[:, :], rs4[:, :], AF.Exp, scale=-0.5).then_inc(a_rs, 1)

            @blk.tensor
            def _(t):
                t.wait_ge(s_pr, 1)
                t.matmul(pbc[:, :], ones_d[:, :], rw_row[:, :],
                         start=True, stop=True).then_inc(s_pr, 1)
                t.wait_ge(d_ct, 32)
                t.wait_ge(d_split, 2)
                for i in range(NT):
                    if i >= 2:
                        t.wait_ge(v_af, i - 1)
                    mm = None
                    band = [(i - d) % NT for d in (30, 31, 0, 1, 2)]
                    n_mm = 0
                    for S in band:
                        d = (i - S) % NT
                        for cc_, aa_ in ((ctl_raw[:, d, 0:128], ach), (ctl_raw[:, d, 128:256], ach),
                                         (ctl_raw[:, d, 0:128], acl)):
                            mm = t.matmul(cv[i % 2][:, :], cc_, aa_[:, S, :],
                                          start=(n_mm == 0), stop=(n_mm == 3 * len(band) - 1))
                            n_mm += 1
                    mm.then_inc(t_cv, 1)
                    t.wait_ge(v_af, i + 1)
                    for u in range(4):
                        gt = 4 * i + u
                        if gt >= 2:
                            t.wait_ge(v_etp, gt - 1)
                        t.transpose(ptp[gt % 2][:, :], af[i % 2][:, u * 128:(u + 1) * 128],
                                    ident[:, :]).then_inc(t_tp, 1)

            @blk.gpsimd
            def _(gp):
                for qq in range(4):
                    gp.wait_ge(d_ag[0], 64 * (4 * qq + 4))
                    gp.wait_ge(d_ag[1], 64 * (4 * qq + 4))
                    gp.collective_compute(
                        "AllGather", ALU.bypass,
                        replica_groups=[[0, 1, 2, 3], [4, 5, 6, 7]],
                        ins=[bass.AP(agi, qq * CH * 1024, [[1024, CH], [1, 1024]]).opt()],
                        outs=[bass.AP(ago, qq * D * 1024, [[1024, D], [1, 1024]]).opt()],
                    ).then_inc(e_cc, 1)
                gp.wait_ge(e_cc, 4)


def _stage_e(nc, g, agi, ago):
    e_cc = g.enter_context(nc.semaphore("e_cc"))
    with nc.Block() as blk:
        @blk.gpsimd
        def _(gp):
            gp.collective_compute(
                "AllGather", ALU.bypass,
                replica_groups=[[0, 1, 2, 3], [4, 5, 6, 7]],
                ins=[agi.ap().opt()], outs=[ago.ap().opt()],
            ).then_inc(e_cc, 1)
            gp.wait_ge(e_cc, 1)


def _stage_f(nc, g, ago, wsplit, xr_e, xres, e_cc):
    wout_h, wout_l = wsplit
    with ExitStack() as st:
        sb = lambda n_, s_: st.enter_context(nc.sbuf_tensor(n_, s_, F32))
        ps = lambda n_, s_: st.enter_context(nc.psum_tensor(n_, s_, F32))
        sem = lambda n_: g.enter_context(nc.semaphore(n_))

        wo_h = [st.enter_context(nc.sbuf_tensor(f"f_woh{i}", [128, 16, 512], BF16))
                for i in range(2)]
        wo_l = [st.enter_context(nc.sbuf_tensor(f"f_wol{i}", [128, 16, 512], BF16))
                for i in range(2)]
        ag_all = sb("f_agall", [128, 16, ROWS])
        agh = st.enter_context(nc.sbuf_tensor("f_agh", [128, 16, ROWS], BF16))
        agl = st.enter_context(nc.sbuf_tensor("f_agl", [128, 16, ROWS], BF16))
        f_spl = g.enter_context(nc.semaphore("f_spl"))
        xrt = [sb(f"f_xrt{i}", [128, 512]) for i in range(2)]
        xo = [sb(f"f_xo{i}", [128, 512]) for i in range(2)]
        pf = [ps(f"f_pf{i}", [128, 512]) for i in range(2)]

        d_wo = [sem("f_dwo0"), sem("f_dwo1")]
        d_ag = e_cc
        d_xr = [sem("f_dxr0"), sem("f_dxr1")]
        d_xo = [sem("f_dxo0"), sem("f_dxo1")]
        t_mm = sem("f_tmm"); v_ev = sem("f_vev")

        NG = 4 * NR

        with nc.Block() as blk:
            @blk.sync
            def _(sync):
                with sync.register("pid") as pid:
                    sync.reg_mov(pid, sync.partition_id())
                    sync.reg_mod(pid, pid, 4)
                    sync.reg_mul(pid, pid, D * 1024)
                    sync.dma_start(
                        out=ag_all[:, :, :],
                        in_=bass.AP(ago, pid, [[1024, 128], [1024 * 128, 16], [1, ROWS]]),
                    ).then_inc(d_ag, 16)
                for mo in range(4):
                    if mo >= 2:
                        sync.wait_ge(t_mm, NR * (mo - 1))
                    sync.dma_start(
                        out=wo_h[mo % 2][:, :, :],
                        in_=bass.AP(wout_h, mo * 512, [[D, 128], [D * 128, 16], [1, 512]]),
                    ).then_inc(d_wo[mo % 2], 16)
                    sync.dma_start(
                        out=wo_l[mo % 2][:, :, :],
                        in_=bass.AP(wout_l, mo * 512, [[D, 128], [D * 128, 16], [1, 512]]),
                    ).then_inc(d_wo[mo % 2], 16)
                    for r in range(NR):
                        g = mo * NR + r
                        if g >= 2:
                            sync.wait_ge(v_ev, g - 1)
                        sync.dma_start(
                            out=xrt[g % 2][:, :],
                            in_=xr_e[r * 128:(r + 1) * 128, mo * 512:(mo + 1) * 512],
                        ).then_inc(d_xr[g % 2], 16)
                        if g >= 1:
                            gp = g - 1
                            mp, rp = gp // NR, gp % NR
                            sync.wait_ge(v_ev, gp + 1)
                            sync.dma_start(
                                out=xres[rp * 128:(rp + 1) * 128, mp * 512:(mp + 1) * 512],
                                in_=xo[gp % 2][:, :],
                            ).then_inc(d_xo[gp % 2], 16)
                gp = NG - 1
                sync.wait_ge(v_ev, NG)
                sync.dma_start(
                    out=xres[(gp % NR) * 128:((gp % NR) + 1) * 128,
                             (gp // NR) * 512:((gp // NR) + 1) * 512],
                    in_=xo[gp % 2][:, :],
                ).then_inc(d_xo[gp % 2], 16)
                sync.wait_ge(d_xo[0], 16 * (NG // 2))
                sync.wait_ge(d_xo[1], 16 * (NG // 2))

            @blk.scalar
            def _(s_):
                s_.wait_ge(d_ag, 20)
                s_.copy(bass.AP(agh, 0, [[16 * ROWS, 128], [1, 16 * ROWS]]),
                        bass.AP(ag_all, 0, [[16 * ROWS, 128], [1, 16 * ROWS]])).then_inc(f_spl, 1)

            @blk.vector
            def _(v):
                v.wait_ge(f_spl, 1)
                v.tensor_tensor(bass.AP(agl, 0, [[16 * ROWS, 128], [1, 16 * ROWS]]),
                                bass.AP(ag_all, 0, [[16 * ROWS, 128], [1, 16 * ROWS]]),
                                bass.AP(agh, 0, [[16 * ROWS, 128], [1, 16 * ROWS]]),
                                op=ALU.subtract).then_inc(f_spl, 1)
                for mo in range(4):
                    for r in range(NR):
                        g = mo * NR + r
                        v.wait_ge(t_mm, g + 1)
                        v.wait_ge(d_xr[g % 2], 16 * (g // 2 + 1))
                        if g >= 2:
                            v.wait_ge(d_xo[g % 2], 16 * (g // 2))
                        v.tensor_tensor(xo[g % 2][:, :], pf[g % 2][:, :],
                                        xrt[g % 2][:, :], op=ALU.add).then_inc(v_ev, 1)

            @blk.tensor
            def _(t):
                t.wait_ge(f_spl, 2)
                for mo in range(4):
                    t.wait_ge(d_wo[mo % 2], 32 * (mo // 2 + 1))
                    for r in range(NR):
                        g = mo * NR + r
                        if g >= 2:
                            t.wait_ge(v_ev, g - 1)
                        mm = None
                        n_mm = 0
                        for c in range(16):
                            for xs_, ws_ in ((agh, wo_h[mo % 2]), (agl, wo_h[mo % 2]),
                                             (agh, wo_l[mo % 2])):
                                mm = t.matmul(pf[g % 2][:, :],
                                              xs_[:, c, r * 128:(r + 1) * 128],
                                              ws_[:, c, :],
                                              start=(n_mm == 0), stop=(n_mm == 47))
                                n_mm += 1
                        mm.then_inc(t_mm, 1)


def _mk_ffn_sems(nc, g):
    sem = lambda n_: g.enter_context(nc.semaphore("ffn_" + n_))
    names = ["dw", "dxg0", "dxg1", "dxo0", "dxo1", "gln", "gact", "gtp", "getp",
             "gh", "ghv", "gnm", "ge8", "geh", "ga", "gup", "gx", "gcast"]
    return {n_: sem(n_) for n_ in names}


def _stage_ffn(nc, sems, ci, xres, wsplit, ident, out_dram=None):
    xdst = out_dram if out_dram is not None else xres
    wf_h, wf_l, wu_h, wu_l = wsplit
    pfx = "gj"[ci]
    with ExitStack() as st:
        sb = lambda n_, s_: st.enter_context(nc.sbuf_tensor(pfx + n_, s_, F32))
        ps = lambda n_, s_: st.enter_context(nc.psum_tensor(pfx + n_, s_, F32))

        wffn_h = st.enter_context(nc.sbuf_tensor(pfx + "wffnh", [128, 16, 136], BF16))
        wffn_l = st.enter_context(nc.sbuf_tensor(pfx + "wffnl", [128, 16, 136], BF16))
        wup_h = st.enter_context(nc.sbuf_tensor(pfx + "wuph", [128, D], BF16))
        wup_l = st.enter_context(nc.sbuf_tensor(pfx + "wupl", [128, D], BF16))
        xg = [sb(f"xg{i}", [128, D]) for i in range(2)]
        lnt = [sb(f"lnt{i}", [128, D]) for i in range(2)]
        lnf_h = [st.enter_context(nc.sbuf_tensor(pfx + f"lnfh{i}", [128, 16, 128], BF16))
                 for i in range(2)]
        lnf_l = [st.enter_context(nc.sbuf_tensor(pfx + f"lnfl{i}", [128, 16, 128], BF16))
                 for i in range(2)]
        bnst = sb("bnst", [128, 24])
        mv = sb("mv", [128, 2])
        rstd = sb("rstd", [128, 1])
        hv = [sb(f"hv{i}", [128, 136]) for i in range(2)]
        gsc = sb("gsc", [128, 8])
        w8 = sb("w8", [128, 8])
        msk8 = sb("msk8", [128, 8])
        gate = sb("gate", [128, 8])
        eh = sb("eh", [128, 128])
        aval = [sb(f"aval{i}", [128, 128]) for i in range(2)]
        atf_h = [st.enter_context(nc.sbuf_tensor(f"{pfx}atfh{i}", [128, 128], BF16))
                 for i in range(2)]
        atf_l = [st.enter_context(nc.sbuf_tensor(f"{pfx}atfl{i}", [128, 128], BF16))
                 for i in range(2)]
        ph = [ps(f"ph{i}", [128, 136]) for i in range(2)]
        ptp = [ps(f"ptp{i}", [128, 128]) for i in range(2)]
        pu = [ps(f"pu{i}", [128, 512]) for i in range(2)]

        d_w = sems["dw"]
        d_xg = [sems["dxg0"], sems["dxg1"]]
        d_xo = [sems["dxo0"], sems["dxo1"]]
        g_ln = sems["gln"]; g_act = sems["gact"]; g_tp = sems["gtp"]; g_etp = sems["getp"]
        g_h = sems["gh"]; g_hv = sems["ghv"]; g_nm = sems["gnm"]; g_e8 = sems["ge8"]
        g_eh = sems["geh"]; g_a = sems["ga"]; g_up = sems["gup"]; g_x = sems["gx"]
        g_cast = sems["gcast"]
        B_LN = ci * NR; B_ACT = ci * NR; B_TP = ci * 17 * NR; B_ETP = ci * 17 * NR
        B_H = ci * NR; B_HV = ci * NR; B_NM = ci * NR; B_E8 = ci * NR
        B_EH = ci * NR; B_A = ci * NR; B_UP = ci * 4 * NR; B_X = ci * 4 * NR
        B_DW = ci * 64; B_DX = ci * 16 * (NR // 2); B_DO = ci * 16 * (NR // 2)
        B_CAST = ci * 17 * NR

        with nc.Block() as blk:
            @blk.sync
            def _(sync):
                for dst_, src_ in ((wffn_h, wf_h), (wffn_l, wf_l)):
                    sync.dma_start(
                        out=dst_[:, :, :],
                        in_=bass.AP(src_, 0, [[136, 128], [136 * 128, 16], [1, 136]]),
                    ).then_inc(d_w, 16)
                sync.dma_start(out=wup_h[:, :], in_=wu_h[:, :]).then_inc(d_w, 16)
                sync.dma_start(out=wup_l[:, :], in_=wu_l[:, :]).then_inc(d_w, 16)
                for r in range(NR):
                    if r >= 2:
                        sync.wait_ge(d_xo[r % 2], B_DO + 16 * (r // 2))
                    sync.dma_start(out=xg[r % 2][:, :],
                                   in_=xres[r * 128:(r + 1) * 128, :]).then_inc(d_xg[r % 2], 16)
                    if r >= 1:
                        sync.wait_ge(g_x, B_X + 4 * r)
                        sync.dma_start(out=xdst[(r - 1) * 128:r * 128, :],
                                       in_=xg[(r - 1) % 2][:, :]).then_inc(d_xo[(r - 1) % 2], 16)
                sync.wait_ge(g_x, B_X + 4 * NR)
                sync.dma_start(out=xdst[(NR - 1) * 128:NR * 128, :],
                               in_=xg[(NR - 1) % 2][:, :]).then_inc(d_xo[(NR - 1) % 2], 16)
                sync.wait_ge(d_xo[0], B_DO + 16 * (NR // 2))
                sync.wait_ge(d_xo[1], B_DO + 16 * (NR // 2))

            @blk.scalar
            def _(s_):
                for r in range(NR):
                    s_.wait_ge(g_ln, B_LN + r + 1)
                    s_.activation(rstd[:, :], rstd[:, :], AF.Ln)
                    s_.drain()
                    s_.activation(rstd[:, :], rstd[:, :], AF.Exp, scale=-0.5).then_inc(g_act, 1)
                    for c in range(16):
                        gt = 17 * r + c
                        s_.wait_ge(g_tp, B_TP + gt + 1)
                        s_.copy(lnf_h[r % 2][:, c, :], ptp[gt % 2][:, :]).then_inc(g_cast, 1)
                    s_.wait_ge(g_nm, B_NM + r + 1)
                    s_.activation(w8[:, :], hv[r % 2][:, 0:8], AF.Exp, bias=gsc[:, 1:2],
                                  accum_out=gsc[:, 2:3]).then_inc(g_e8, 1)
                    s_.activation(eh[:, :], hv[r % 2][:, 8:136], AF.Exp,
                                  scale=-1.0).then_inc(g_eh, 1)
                    gt = 17 * r + 16
                    s_.wait_ge(g_tp, B_TP + gt + 1)
                    s_.copy(atf_h[r % 2][:, :], ptp[gt % 2][:, :]).then_inc(g_cast, 1)

            @blk.vector
            def _(v):
                for r in range(NR):
                    v.wait_ge(d_xg[r % 2], B_DX + 16 * (r // 2 + 1))
                    for c in range(4):
                        v.bn_stats(bnst[:, 6 * c:6 * (c + 1)], xg[r % 2][:, 512 * c:512 * (c + 1)])
                    v.drain()
                    v.bn_aggr(mv[:, :], bnst[:, :])
                    v.drain()
                    v.tensor_scalar(rstd[:, :], mv[:, 1:2], EPS, None,
                                    op0=ALU.add).then_inc(g_ln, 1)
                    v.wait_ge(g_act, B_ACT + r + 1)
                    if r >= 2:
                        v.wait_ge(g_hv, B_HV + r - 1)
                    v.tensor_scalar(lnt[r % 2][:, :], xg[r % 2][:, :], mv[:, 0:1], rstd[:, 0:1],
                                    op0=ALU.subtract, op1=ALU.mult).then_inc(g_h, 1)
                    for c in range(16):
                        gt = 17 * r + c
                        v.wait_ge(g_cast, B_CAST + 17 * r + c + 1)
                        v.tensor_tensor(lnf_l[r % 2][:, c, :], ptp[gt % 2][:, :],
                                        lnf_h[r % 2][:, c, :],
                                        op=ALU.subtract).then_inc(g_etp, 1)
                    v.wait_ge(g_hv, B_HV + r + 1)
                    v.tensor_copy(hv[r % 2][:, :], ph[r % 2][:, :])
                    v.drain()
                    v.reduce_max(gsc[:, 0:1], hv[r % 2][:, 0:8], axis=AX.X)
                    v.drain()
                    v.tensor_scalar(gsc[:, 1:2], gsc[:, 0:1], -1.0, None,
                                    op0=ALU.mult).then_inc(g_nm, 1)
                    v.wait_ge(g_e8, B_E8 + r + 1)
                    v.reciprocal(gsc[:, 3:4], gsc[:, 2:3])
                    v.drain()
                    v.tensor_scalar(w8[:, :], w8[:, :], gsc[:, 3:4], None, op0=ALU.mult)
                    v.drain()
                    v.reduce_max(gsc[:, 4:5], w8[:, :], axis=AX.X)
                    v.drain()
                    v.tensor_scalar(msk8[:, :], w8[:, :], gsc[:, 4:5], None, op0=ALU.is_ge)
                    v.drain()
                    v.scalar_tensor_tensor(msk8[:, :], msk8[:, :], -10.0, w8[:, :],
                                           op0=ALU.mult, op1=ALU.add)
                    v.drain()
                    v.reduce_max(gsc[:, 5:6], msk8[:, :], axis=AX.X)
                    v.drain()
                    v.tensor_scalar(msk8[:, :], w8[:, :], gsc[:, 5:6], None, op0=ALU.is_ge)
                    v.drain()
                    v.tensor_tensor(gate[:, :], w8[:, :], msk8[:, :], op=ALU.mult)
                    v.wait_ge(g_eh, B_EH + r + 1)
                    v.tensor_scalar(eh[:, :], eh[:, :], 1.0, None, op0=ALU.add)
                    v.drain()
                    v.reciprocal(eh[:, :], eh[:, :])
                    v.drain()
                    if r >= 2:
                        v.wait_ge(g_tp, B_TP + 17 * (r - 1) + 17)
                    v.tensor_tensor(aval[r % 2][:, :], hv[r % 2][:, 8:136], eh[:, :],
                                    op=ALU.mult)
                    v.drain()
                    for e in range(8):
                        v.tensor_scalar(aval[r % 2][:, e * 16:(e + 1) * 16],
                                        aval[r % 2][:, e * 16:(e + 1) * 16],
                                        gate[:, e:e + 1], None, op0=ALU.mult)
                    v.drain()
                    v.nop().then_inc(g_a, 1)
                    gt = 17 * r + 16
                    v.wait_ge(g_cast, B_CAST + 17 * r + 17)
                    v.tensor_tensor(atf_l[r % 2][:, :], ptp[gt % 2][:, :],
                                    atf_h[r % 2][:, :], op=ALU.subtract).then_inc(g_etp, 1)
                    for sp in range(4):
                        v.wait_ge(g_up, B_UP + 4 * r + sp + 1)
                        v.tensor_tensor(xg[r % 2][:, sp * 512:(sp + 1) * 512],
                                        xg[r % 2][:, sp * 512:(sp + 1) * 512],
                                        pu[sp % 2][:, :], op=ALU.add).then_inc(g_x, 1)

            @blk.tensor
            def _(t):
                t.wait_ge(d_w, B_DW + 64)
                for r in range(NR):
                    t.wait_ge(g_h, B_H + r + 1)
                    for c in range(16):
                        gt = 17 * r + c
                        if gt >= 2:
                            t.wait_ge(g_etp, B_ETP + gt - 1)
                        t.transpose(ptp[gt % 2][:, :], lnt[r % 2][:, c * 128:(c + 1) * 128],
                                    ident[:, :]).then_inc(g_tp, 1)
                    t.wait_ge(g_etp, B_ETP + 17 * r + 16)
                    if r >= 2:
                        t.wait_ge(g_nm, B_NM + r - 1)
                    mm = None
                    n_mm = 0
                    for c in range(16):
                        for xs_, ws_ in ((lnf_h[r % 2], wffn_h), (lnf_l[r % 2], wffn_h),
                                         (lnf_h[r % 2], wffn_l)):
                            mm = t.matmul(ph[r % 2][:, :], xs_[:, c, :], ws_[:, c, :],
                                          start=(n_mm == 0), stop=(n_mm == 47))
                            n_mm += 1
                    mm.then_inc(g_hv, 1)
                    t.wait_ge(g_a, B_A + r + 1)
                    gt = 17 * r + 16
                    if gt >= 2:
                        t.wait_ge(g_etp, B_ETP + gt - 1)
                    t.transpose(ptp[gt % 2][:, :], aval[r % 2][:, :],
                                ident[:, :]).then_inc(g_tp, 1)
                    t.wait_ge(g_etp, B_ETP + gt + 1)
                    for sp in range(4):
                        gg = 4 * r + sp
                        if gg >= 2:
                            t.wait_ge(g_x, B_X + gg - 1)
                        mm = None
                        for xs_, ws_ in ((atf_h[r % 2], wup_h), (atf_l[r % 2], wup_h),
                                         (atf_h[r % 2], wup_l)):
                            mm = t.matmul(pu[sp % 2][:, :], xs_[:, :],
                                          ws_[:, sp * 512:(sp + 1) * 512],
                                          start=(xs_ is atf_h[r % 2] and ws_ is wup_h),
                                          stop=(ws_ is wup_l))
                        mm.then_inc(g_up, 1)


def _stage_h(nc, g, xres, wsplit, bin_e, bout_e, lnw_e, lnb_e, ropeb_e,
             ident, ari, aro):
    wi_h, wi_l, wb_h, wb_l = wsplit
    with ExitStack() as st:
        sb = lambda n_, s_: st.enter_context(nc.sbuf_tensor("h" + n_, s_, F32))
        ps = lambda n_, s_: st.enter_context(nc.psum_tensor("h" + n_, s_, F32))

        lnw_r = sb("lnwr", [128, D])
        lnb_r = sb("lnbr", [128, D])
        binr = sb("binr", [128, DFF])
        boutr = sb("boutr", [128, D])
        arep = sb("arep", [128, D])
        onec = sb("onec", [128, 1])
        oner = sb("oner", [1, 128])
        wt_h = [st.enter_context(nc.sbuf_tensor(f"hwth{i}", [128, 16, 512], BF16))
                for i in range(2)]
        wt_l = [st.enter_context(nc.sbuf_tensor(f"hwtl{i}", [128, 16, 512], BF16))
                for i in range(2)]
        xcur = [sb(f"xc{i}", [128, D]) for i in range(2)]
        lnt = sb("lnt", [128, D])
        ht = sb("ht", [128, D])
        hf_h = st.enter_context(nc.sbuf_tensor("hhfh", [128, 16, 128], BF16))
        hf_l = st.enter_context(nc.sbuf_tensor("hhfl", [128, 16, 128], BF16))
        bnst = sb("bnst", [128, 24])
        mv = sb("mv", [128, 2])
        rstd = sb("rstd", [128, 1])
        kt4 = sb("kt4", [128, 4, 512])
        vt2 = sb("vt2", [128, 512])
        sqd = sb("sqd", [128, 512])
        psb = [sb(f"psb{i}", [128, 512]) for i in range(2)]
        rc2 = sb("rc2", [128, 1024])
        rlo = sb("rlo", [128, 1024])
        rtmp = sb("rtmp", [128, 512])
        rtmp2 = sb("rtmp2", [128, 512])
        acc4 = sb("acc4", [128, 4])
        ks = sb("ks", [128, 1])
        yf_h = st.enter_context(nc.sbuf_tensor("hyfh", [128, 16, 128], BF16))
        yf_l = st.enter_context(nc.sbuf_tensor("hyfl", [128, 16, 128], BF16))
        pp = [ps(f"pp{i}", [128, 512]) for i in range(2)]
        ptp = [ps(f"ptp{i}", [128, 128]) for i in range(2)]
        pa = [ps(f"pa{i}", [128, 512]) for i in range(4)]

        h0d = g.enter_context(nc.semaphore("h0d"))
        h0v = g.enter_context(nc.semaphore("h0v"))
        h0m = g.enter_context(nc.semaphore("h0m"))
        with nc.Block() as blk:
            @blk.sync
            def _(sync):
                sync.dma_start(out=binr[0:1, :], in_=bin_e[:, :]).then_inc(h0d, 16)
                sync.dma_start(out=boutr[0:1, :], in_=bout_e[:, :]).then_inc(h0d, 16)
                sync.dma_start(out=lnw_r[0:1, :], in_=lnw_e[:, :]).then_inc(h0d, 16)
                sync.dma_start(out=lnb_r[0:1, :], in_=lnb_e[:, :]).then_inc(h0d, 16)

            @blk.vector
            def _(v):
                v.memset(onec[:, :], 1.0)
                v.memset(oner[:, :], 1.0)
                v.drain()
                v.nop().then_inc(h0v, 1)
                for i in range(24):
                    v.wait_ge(h0m, i + 1)
                    if i < 12:
                        dst = binr[:, i * 512:(i + 1) * 512]
                    elif i < 16:
                        dst = boutr[:, (i - 12) * 512:(i - 11) * 512]
                    elif i < 20:
                        dst = lnw_r[:, (i - 16) * 512:(i - 15) * 512]
                    else:
                        dst = lnb_r[:, (i - 20) * 512:(i - 19) * 512]
                    v.tensor_copy(dst, pa[i % 4][:, :]).then_inc(h0v, 1)

            @blk.tensor
            def _(t):
                t.wait_ge(h0d, 64)
                t.wait_ge(h0v, 1)
                for i in range(24):
                    if i >= 4:
                        t.wait_ge(h0v, i - 2)
                    if i < 12:
                        srow = binr[0:1, i * 512:(i + 1) * 512]
                    elif i < 16:
                        srow = boutr[0:1, (i - 12) * 512:(i - 11) * 512]
                    elif i < 20:
                        srow = lnw_r[0:1, (i - 16) * 512:(i - 15) * 512]
                    else:
                        srow = lnb_r[0:1, (i - 20) * 512:(i - 19) * 512]
                    t.matmul(pa[i % 4][:, :], oner[:, :], srow,
                             start=True, stop=True).then_inc(h0m, 1)

        with ExitStack() as st1:
            sm = lambda n_: g.enter_context(nc.semaphore(n_))
            blk = st1.enter_context(nc.Block())
            d_w = [sm("h1dw0"), sm("h1dw1")]
            d_x = [sm("h1dx0"), sm("h1dx1")]
            d_r = [sm("h1dr0")]
            h_ln = sm("h1ln"); h_act = sm("h1act"); h_ht = sm("h1ht")
            h_tp = sm("h1tp"); h_etp = sm("h1etp"); h_mm = sm("h1mm")
            h_ppc = sm("h1ppc"); h_rope = sm("h1rope"); h_sq = sm("h1sq")
            h_ks = sm("h1ks"); h_rs = sm("h1rs"); h_kn = sm("h1kn")
            h_pv = sm("h1pv"); h_pa = sm("h1pa"); h_cast = sm("h1cast")
            h1_sems = dict(d_w=d_w, d_x=d_x, d_r=d_r, h_ln=h_ln, h_act=h_act,
                           h_ht=h_ht, h_tp=h_tp, h_etp=h_etp, h_mm=h_mm,
                           h_ppc=h_ppc, h_rope=h_rope, h_cast=h_cast,
                           h_sq=h_sq, h_ks=h_ks, h_kn=h_kn, h_pv=h_pv)

            @blk.sync
            def _(sync):
                for r in range(NR):
                    if r >= 2:
                        sync.wait_ge(h_ht, r - 1)
                    sync.dma_start(out=xcur[r % 2][:, :],
                                   in_=xres[r * 128:(r + 1) * 128, :]).then_inc(d_x[r % 2], 16)
                    if r >= 1:
                        sync.wait_ge(h_rope, r)
                    sync.dma_start(out=rc2[:, :],
                                   in_=ropeb_e[r * 128:(r + 1) * 128, 0:1024]).then_inc(d_r[0], 16)
                    sync.dma_start(out=rlo[:, :],
                                   in_=ropeb_e[r * 128:(r + 1) * 128, 1024:2048]).then_inc(d_r[0], 16)
                    for sp in range(8):
                        w_i = r * 8 + sp
                        if w_i >= 2:
                            sync.wait_ge(h_mm, w_i - 1)
                        for dst_, src_ in ((wt_h, wi_h), (wt_l, wi_l)):
                            sync.dma_start(
                                out=dst_[w_i % 2][:, :, :],
                                in_=bass.AP(src_, (4 + sp) * 512,
                                            [[DFF, 128], [DFF * 128, 16], [1, 512]]),
                            ).then_inc(d_w[w_i % 2], 16)

            @blk.scalar
            def _(s_):
                for r in range(NR):
                    s_.wait_ge(h_ln, r + 1)
                    s_.activation(rstd[:, :], rstd[:, :], AF.Ln)
                    s_.drain()
                    s_.activation(rstd[:, :], rstd[:, :], AF.Exp, scale=-0.5).then_inc(h_act, 1)
                    for c in range(16):
                        gt = 16 * r + c
                        s_.wait_ge(h_tp, gt + 1)
                        s_.copy(hf_h[:, c, :], ptp[gt % 2][:, :]).then_inc(h_cast, 1)
                    s_.wait_ge(h_rope, r + 1)
                    if r >= 1:
                        s_.wait_ge(h_ks, r)
                    for sp in range(4):
                        s_.activation(sqd[:, :], kt4[:, sp, :], AF.Square,
                                      accum_out=acc4[:, sp:sp + 1])
                        s_.drain()
                    s_.nop().then_inc(h_sq, 1)
                    s_.wait_ge(h_ks, r + 1)
                    s_.activation(ks[:, :], ks[:, :], AF.Ln)
                    s_.drain()
                    s_.activation(ks[:, :], ks[:, :], AF.Exp, scale=-0.5).then_inc(h_rs, 1)

            @blk.vector
            def _(v):
                for r in range(NR):
                    v.wait_ge(d_x[r % 2], 16 * (r // 2 + 1))
                    if r >= 1:
                        v.wait_ge(h_mm, 8 * r)
                    for c in range(4):
                        v.bn_stats(bnst[:, 6 * c:6 * (c + 1)],
                                   xcur[r % 2][:, 512 * c:512 * (c + 1)])
                    v.drain()
                    v.bn_aggr(mv[:, :], bnst[:, :])
                    v.drain()
                    v.tensor_scalar(rstd[:, :], mv[:, 1:2], EPS, None,
                                    op0=ALU.add).then_inc(h_ln, 1)
                    v.wait_ge(h_act, r + 1)
                    v.tensor_scalar(lnt[:, :], xcur[r % 2][:, :], mv[:, 0:1], rstd[:, 0:1],
                                    op0=ALU.subtract, op1=ALU.mult)
                    v.drain()
                    v.tensor_tensor(ht[:, :], lnt[:, :], lnw_r[:, :], op=ALU.mult)
                    v.drain()
                    v.tensor_tensor(ht[:, :], ht[:, :], lnb_r[:, :], op=ALU.add).then_inc(h_ht, 1)
                    for c in range(16):
                        gt = 16 * r + c
                        v.wait_ge(h_cast, 16 * r + c + 1)
                        v.tensor_tensor(hf_l[:, c, :], ptp[gt % 2][:, :], hf_h[:, c, :],
                                        op=ALU.subtract).then_inc(h_etp, 1)
                    for sp in range(4):
                        v.wait_ge(h_mm, 8 * r + sp + 1)
                        v.tensor_tensor(kt4[:, sp, :], pp[sp % 2][:, :],
                                        binr[:, 2048 + sp * 512:2048 + (sp + 1) * 512],
                                        op=ALU.add).then_inc(h_ppc, 1)
                    v.wait_ge(d_r[0], 32 * (r + 1))
                    v.tensor_tensor(rc2[:, :], rc2[:, :], rlo[:, :], op=ALU.add)
                    v.drain()
                    kev = bass.AP(kt4, 0, [[2048, 128], [2, 512]])
                    kod = bass.AP(kt4, 1, [[2048, 128], [2, 512]])
                    cos, sin = rc2[:, 0:512], rc2[:, 512:1024]
                    v.tensor_tensor(rtmp[:, :], kod, sin, op=ALU.mult)
                    v.tensor_tensor(rtmp2[:, :], kev, sin, op=ALU.mult)
                    v.drain()
                    v.tensor_tensor(kev, kev, cos, op=ALU.mult)
                    v.tensor_tensor(kod, kod, cos, op=ALU.mult)
                    v.drain()
                    v.tensor_tensor(kev, kev, rtmp[:, :], op=ALU.subtract)
                    v.tensor_tensor(kod, kod, rtmp2[:, :], op=ALU.add)
                    v.drain()
                    v.nop().then_inc(h_rope, 1)
                    v.wait_ge(h_sq, r + 1)
                    v.reduce_sum(ks[:, :], acc4[:, :], axis=AX.X).then_inc(h_ks, 1)
                    v.wait_ge(h_rs, r + 1)
                    v.tensor_scalar(bass.AP(kt4, 0, [[2048, 128], [1, 2048]]),
                                    bass.AP(kt4, 0, [[2048, 128], [1, 2048]]),
                                    ks[:, 0:1], None, op0=ALU.mult)
                    v.drain()
                    v.nop().then_inc(h_kn, 1)
                    for sv in range(4):
                        j = 4 * r + sv
                        v.wait_ge(h_mm, 8 * r + 4 + sv + 1)
                        v.tensor_tensor(vt2[:, :], pp[sv % 2][:, :],
                                        binr[:, 4096 + sv * 512:4096 + (sv + 1) * 512],
                                        op=ALU.add)
                        v.drain()
                        if j >= 2:
                            v.wait_ge(h_pa, j - 1)
                        v.tensor_tensor(psb[j % 2][:, :], vt2[:, :], kt4[:, sv, :],
                                        op=ALU.mult).then_inc(h_pv, 1)
                        v.drain()
                        v.nop().then_inc(h_ppc, 1)

            @blk.tensor
            def _(t):
                for r in range(NR):
                    t.wait_ge(h_ht, r + 1)
                    for c in range(16):
                        gt = 16 * r + c
                        if gt >= 2:
                            t.wait_ge(h_etp, gt - 1)
                        t.transpose(ptp[gt % 2][:, :], ht[:, c * 128:(c + 1) * 128],
                                    ident[:, :]).then_inc(h_tp, 1)
                    t.wait_ge(h_etp, 16 * (r + 1))
                    for sp in range(8):
                        w_i = r * 8 + sp
                        g = w_i
                        t.wait_ge(d_w[w_i % 2], 32 * (w_i // 2 + 1))
                        if g >= 2:
                            t.wait_ge(h_ppc, g - 1)
                        if sp >= 4:
                            t.wait_ge(h_kn, r + 1)
                        mm = None
                        n_mm = 0
                        for c in range(16):
                            for xs_, ws_ in ((hf_h, wt_h[w_i % 2]), (hf_l, wt_h[w_i % 2]),
                                             (hf_h, wt_l[w_i % 2])):
                                mm = t.matmul(pp[sp % 2][:, :], xs_[:, c, :], ws_[:, c, :],
                                              start=(n_mm == 0), stop=(n_mm == 47))
                                n_mm += 1
                        mm.then_inc(h_mm, 1)
                        if sp >= 4:
                            sv = sp - 4
                            j = 4 * r + sv
                            t.wait_ge(h_pv, j + 1)
                            t.matmul(pa[sv][0:1, :], onec[:, :], psb[j % 2][:, :],
                                     start=(r == 0), stop=(r == NR - 1)).then_inc(h_pa, 1)

        h2s = h0v; B2S = 25
        h2d = h0d; B2D = 64
        h2cc = h0m; B2CC = 24
        h2m = h1_sems["h_sq"]; B2M = NR
        with nc.Block() as blk:
            @blk.vector
            def _(v):
                for sv in range(4):
                    v.tensor_copy(arep[0:1, sv * 512:(sv + 1) * 512],
                                  pa[sv][0:1, :]).then_inc(h2s, 1)
                for sv in range(4):
                    v.wait_ge(h2m, B2M + sv + 1)
                    v.tensor_copy(arep[:, sv * 512:(sv + 1) * 512], pa[sv][:, :])

            @blk.sync
            def _(sync):
                sync.wait_ge(h2s, B2S + 4)
                sync.dma_start(out=ari[:, :], in_=arep[0:1, :]).then_inc(h2d, 16)

            @blk.gpsimd
            def _(gp):
                gp.wait_ge(h2d, B2D + 16)
                gp.collective_compute(
                    "AllReduce", ALU.add,
                    replica_groups=[[0, 1, 2, 3], [4, 5, 6, 7]],
                    ins=[ari.ap().opt()], outs=[aro.ap().opt()],
                ).then_inc(h2cc, 1)
                gp.wait_ge(h2cc, B2CC + 1)

            @blk.sync
            def _(sync):
                sync.wait_ge(h2cc, B2CC + 1)
                sync.dma_start(out=arep[0:1, :], in_=aro[:, :]).then_inc(h2d, 16)

            @blk.tensor
            def _(t):
                t.wait_ge(h2d, B2D + 32)
                for sv in range(4):
                    t.matmul(pa[sv][:, :], oner[:, :], arep[0:1, sv * 512:(sv + 1) * 512],
                             start=True, stop=True).then_inc(h2m, 1)

        with ExitStack() as st3:
            sm = lambda n_: g.enter_context(nc.semaphore(n_))
            blk = st3.enter_context(nc.Block())
            d_w = h1_sems["d_w"]; d_x = h1_sems["d_x"]; d_r = h1_sems["d_r"]
            d_o = [h1_sems["h_ks"], h1_sems["h_kn"]]
            BDO = [NR, NR]
            h_ln = h1_sems["h_ln"]; h_act = h1_sems["h_act"]; h_ht = h1_sems["h_ht"]
            h_tp = h1_sems["h_tp"]; h_etp = h1_sems["h_etp"]; h_mm = h1_sems["h_mm"]
            h_ppc = h1_sems["h_ppc"]; h_rope = h1_sems["h_rope"]
            h_cast = h1_sems["h_cast"]
            h_y = h1_sems["h_sq"]; BY = NR + 4
            h_x = h1_sems["h_pv"]; BX = 4 * NR
            BLN = NR; BACT = NR; BHT = NR; BTP = 16 * NR; BETP = 16 * NR
            BMM = 8 * NR; BPPC = 8 * NR; BROPE = NR; BCAST = 16 * NR
            BDW = [32 * 32, 32 * 32]; BDX = [16 * (NR // 2), 16 * (NR // 2)]
            BDR = 16 * 2 * NR

            @blk.sync
            def _(sync):
                for r in range(NR):
                    if r >= 2:
                        sync.wait_ge(d_o[r % 2], BDO[r % 2] + 16 * (r // 2))
                    sync.dma_start(out=xcur[r % 2][:, :],
                                   in_=xres[r * 128:(r + 1) * 128, :]).then_inc(d_x[r % 2], 16)
                    if r >= 1:
                        sync.wait_ge(h_rope, BROPE + r)
                    sync.dma_start(out=rc2[:, :],
                                   in_=ropeb_e[r * 128:(r + 1) * 128, 0:1024]).then_inc(d_r[0], 16)
                    sync.dma_start(out=rlo[:, :],
                                   in_=ropeb_e[r * 128:(r + 1) * 128, 1024:2048]).then_inc(d_r[0], 16)
                    for sp in range(8):
                        w_i = r * 8 + sp
                        if w_i >= 2:
                            sync.wait_ge(h_mm, BMM + w_i - 1)
                        if sp < 4:
                            aph = bass.AP(wi_h, sp * 512, [[DFF, 128], [DFF * 128, 16], [1, 512]])
                            apl = bass.AP(wi_l, sp * 512, [[DFF, 128], [DFF * 128, 16], [1, 512]])
                        else:
                            aph = bass.AP(wb_h, (sp - 4) * 512, [[D, 128], [D * 128, 16], [1, 512]])
                            apl = bass.AP(wb_l, (sp - 4) * 512, [[D, 128], [D * 128, 16], [1, 512]])
                        sync.dma_start(out=wt_h[w_i % 2][:, :, :], in_=aph).then_inc(d_w[w_i % 2], 16)
                        sync.dma_start(out=wt_l[w_i % 2][:, :, :], in_=apl).then_inc(d_w[w_i % 2], 16)
                    if r >= 1:
                        sync.wait_ge(h_x, BX + 4 * r)
                        sync.dma_start(out=xres[(r - 1) * 128:r * 128, :],
                                       in_=xcur[(r - 1) % 2][:, :]).then_inc(d_o[(r - 1) % 2], 16)
                sync.wait_ge(h_x, BX + 4 * NR)
                sync.dma_start(out=xres[(NR - 1) * 128:NR * 128, :],
                               in_=xcur[(NR - 1) % 2][:, :]).then_inc(d_o[(NR - 1) % 2], 16)
                sync.wait_ge(d_o[0], BDO[0] + 16 * (NR // 2))
                sync.wait_ge(d_o[1], BDO[1] + 16 * (NR // 2))

            @blk.scalar
            def _(s_):
                for r in range(NR):
                    s_.wait_ge(h_ln, BLN + r + 1)
                    s_.activation(rstd[:, :], rstd[:, :], AF.Ln)
                    s_.drain()
                    s_.activation(rstd[:, :], rstd[:, :], AF.Exp, scale=-0.5).then_inc(h_act, 1)
                    for c in range(16):
                        gt = 32 * r + c
                        s_.wait_ge(h_tp, BTP + gt + 1)
                        s_.copy(hf_h[:, c, :], ptp[gt % 2][:, :]).then_inc(h_cast, 1)
                    for c in range(16):
                        gt = 32 * r + 16 + c
                        s_.wait_ge(h_tp, BTP + gt + 1)
                        s_.copy(yf_h[:, c, :], ptp[gt % 2][:, :]).then_inc(h_cast, 1)

            @blk.vector
            def _(v):
                for r in range(NR):
                    v.wait_ge(d_x[r % 2], BDX[r % 2] + 16 * (r // 2 + 1))
                    if r >= 1:
                        v.wait_ge(h_mm, BMM + 8 * r)
                    for c in range(4):
                        v.bn_stats(bnst[:, 6 * c:6 * (c + 1)],
                                   xcur[r % 2][:, 512 * c:512 * (c + 1)])
                    v.drain()
                    v.bn_aggr(mv[:, :], bnst[:, :])
                    v.drain()
                    v.tensor_scalar(rstd[:, :], mv[:, 1:2], EPS, None,
                                    op0=ALU.add).then_inc(h_ln, 1)
                    v.wait_ge(h_act, BACT + r + 1)
                    v.tensor_scalar(lnt[:, :], xcur[r % 2][:, :], mv[:, 0:1], rstd[:, 0:1],
                                    op0=ALU.subtract, op1=ALU.mult)
                    v.drain()
                    v.tensor_tensor(ht[:, :], lnt[:, :], lnw_r[:, :], op=ALU.mult)
                    v.drain()
                    v.tensor_tensor(ht[:, :], ht[:, :], lnb_r[:, :], op=ALU.add).then_inc(h_ht, 1)
                    for c in range(16):
                        gt = 32 * r + c
                        v.wait_ge(h_cast, BCAST + 32 * r + c + 1)
                        v.tensor_tensor(hf_l[:, c, :], ptp[gt % 2][:, :], hf_h[:, c, :],
                                        op=ALU.subtract).then_inc(h_etp, 1)
                    for sp in range(4):
                        v.wait_ge(h_mm, BMM + 8 * r + sp + 1)
                        v.tensor_tensor(kt4[:, sp, :], pp[sp % 2][:, :],
                                        binr[:, sp * 512:(sp + 1) * 512],
                                        op=ALU.add).then_inc(h_ppc, 1)
                    v.wait_ge(d_r[0], BDR + 32 * (r + 1))
                    v.tensor_tensor(rc2[:, :], rc2[:, :], rlo[:, :], op=ALU.add)
                    v.drain()
                    qev = bass.AP(kt4, 0, [[2048, 128], [2, 512]])
                    qod = bass.AP(kt4, 1, [[2048, 128], [2, 512]])
                    cos, sin = rc2[:, 0:512], rc2[:, 512:1024]
                    v.tensor_tensor(rtmp[:, :], qod, sin, op=ALU.mult)
                    v.tensor_tensor(rtmp2[:, :], qev, sin, op=ALU.mult)
                    v.drain()
                    v.tensor_tensor(qev, qev, cos, op=ALU.mult)
                    v.tensor_tensor(qod, qod, cos, op=ALU.mult)
                    v.drain()
                    v.tensor_tensor(qev, qev, rtmp[:, :], op=ALU.subtract)
                    v.tensor_tensor(qod, qod, rtmp2[:, :], op=ALU.add)
                    v.drain()
                    v.nop().then_inc(h_rope, 1)
                    v.tensor_tensor(bass.AP(kt4, 0, [[2048, 128], [1, 2048]]),
                                    bass.AP(kt4, 0, [[2048, 128], [1, 2048]]),
                                    arep[:, :], op=ALU.mult)
                    v.drain()
                    v.nop().then_inc(h_y, 1)
                    for c in range(16):
                        gt = 32 * r + 16 + c
                        v.wait_ge(h_cast, BCAST + 32 * r + 16 + c + 1)
                        v.tensor_tensor(yf_l[:, c, :], ptp[gt % 2][:, :], yf_h[:, c, :],
                                        op=ALU.subtract).then_inc(h_etp, 1)
                    for sp in range(4):
                        v.wait_ge(h_mm, BMM + 8 * r + 4 + sp + 1)
                        v.tensor_tensor(rtmp[:, :], pp[sp % 2][:, :],
                                        boutr[:, sp * 512:(sp + 1) * 512], op=ALU.add)
                        v.drain()
                        v.tensor_tensor(rtmp[:, :], rtmp[:, :],
                                        lnt[:, sp * 512:(sp + 1) * 512], op=ALU.add)
                        v.drain()
                        v.tensor_tensor(xcur[r % 2][:, sp * 512:(sp + 1) * 512],
                                        xcur[r % 2][:, sp * 512:(sp + 1) * 512],
                                        rtmp[:, :], op=ALU.add)
                        v.drain()
                        v.nop().then_inc(h_ppc, 1)
                        v.nop().then_inc(h_x, 1)

            @blk.tensor
            def _(t):
                for r in range(NR):
                    t.wait_ge(h_ht, BHT + r + 1)
                    for c in range(16):
                        gt = 32 * r + c
                        if gt >= 2:
                            t.wait_ge(h_etp, BETP + gt - 1)
                        t.transpose(ptp[gt % 2][:, :], ht[:, c * 128:(c + 1) * 128],
                                    ident[:, :]).then_inc(h_tp, 1)
                    t.wait_ge(h_etp, BETP + 32 * r + 16)
                    for sp in range(8):
                        w_i = r * 8 + sp
                        t.wait_ge(d_w[w_i % 2], BDW[w_i % 2] + 32 * (w_i // 2 + 1))
                        if w_i >= 2:
                            t.wait_ge(h_ppc, BPPC + w_i - 1)
                        if sp == 4:
                            t.wait_ge(h_y, BY + r + 1)
                            for c in range(16):
                                gt = 32 * r + 16 + c
                                if gt >= 2:
                                    t.wait_ge(h_etp, BETP + gt - 1)
                                t.transpose(ptp[gt % 2][:, :],
                                            bass.AP(kt4, c * 128, [[2048, 128], [1, 128]]),
                                            ident[:, :]).then_inc(h_tp, 1)
                            t.wait_ge(h_etp, BETP + 32 * r + 32)
                        sf_h, sf_l = (hf_h, hf_l) if sp < 4 else (yf_h, yf_l)
                        mm = None
                        n_mm = 0
                        for c in range(16):
                            for xs_, ws_ in ((sf_h, wt_h[w_i % 2]), (sf_l, wt_h[w_i % 2]),
                                             (sf_h, wt_l[w_i % 2])):
                                mm = t.matmul(pp[sp % 2][:, :], xs_[:, c, :], ws_[:, c, :],
                                              start=(n_mm == 0), stop=(n_mm == 47))
                                n_mm += 1
                        mm.then_inc(h_mm, 1)


def make_inputs(inputs):
    X = np.ascontiguousarray(np.asarray(inputs["L0_attn_a_query"], dtype=np.float32))
    Wq = np.asarray(inputs["L0_attn_a_Wq"], np.float32)
    Wk = np.asarray(inputs["L0_attn_a_Wk"], np.float32)
    Wv = np.asarray(inputs["L0_attn_a_Wv"], np.float32)
    Wout = np.ascontiguousarray(np.asarray(inputs["L0_attn_a_Wout"], np.float32))
    lam = np.concatenate([np.asarray(inputs[k], np.float32).ravel() for k in
                          ("L0_attn_a_lambda_q1", "L0_attn_a_lambda_k1",
                           "L0_attn_a_lambda_q2", "L0_attn_a_lambda_k2")])[None, :]
    rmsw = np.asarray(inputs["L0_attn_a_rms_weight"], np.float32)[None, :]
    lnw = np.asarray(inputs["L0_attn_b_ln_w"], np.float32)[None, :]
    lnb = np.asarray(inputs["L0_attn_b_ln_b"], np.float32)[None, :]
    Win = np.ascontiguousarray(np.asarray(inputs["L0_attn_b_W_in"], np.float32))
    bin_ = np.asarray(inputs["L0_attn_b_b_in"], np.float32)[None, :]
    Woutb = np.ascontiguousarray(np.asarray(inputs["L0_attn_b_W_out"], np.float32))
    bout = np.asarray(inputs["L0_attn_b_b_out"], np.float32)[None, :]
    router = np.asarray(inputs["L0_ffn_router_w"], np.float32)
    down = np.asarray(inputs["L0_ffn_down_w"], np.float32)
    up = np.asarray(inputs["L0_ffn_up_w"], np.float32)

    wffn = np.concatenate([router.T, down.reshape(128, D).T], axis=1)
    wffn = np.ascontiguousarray(wffn)
    wup = np.ascontiguousarray(up.transpose(0, 2, 1).reshape(128, D))

    tt = np.arange(T, dtype=np.float64)
    inva = 10000.0 ** (-np.arange(0, 64, 2, dtype=np.float64) / 64.0)
    anga = np.outer(tt, inva)
    cos_a = np.tile(np.cos(anga), (1, 8))
    sin_a = np.tile(np.sin(anga), (1, 8))
    ropea = np.empty((T, 1024), np.float32)
    ropea[:, 0:256] = cos_a; ropea[:, 256:512] = sin_a
    ropea[:, 512:768] = (cos_a - ropea[:, 0:256].astype(np.float64)).astype(np.float32)
    ropea[:, 768:1024] = (sin_a - ropea[:, 256:512].astype(np.float64)).astype(np.float32)

    invb = 10000.0 ** (-np.arange(0, 1024, 2, dtype=np.float64) / 1024.0)
    angb = np.outer(tt, invb)
    cos_b, sin_b = np.cos(angb), np.sin(angb)
    ropeb_full = np.empty((T, 2048), np.float32)
    ropeb_full[:, 0:512] = cos_b; ropeb_full[:, 512:1024] = sin_b
    ropeb_full[:, 1024:1536] = (cos_b - ropeb_full[:, 0:512].astype(np.float64)).astype(np.float32)
    ropeb_full[:, 1536:2048] = (sin_b - ropeb_full[:, 512:1024].astype(np.float64)).astype(np.float32)

    nfreq = T // 2 + 1
    xs = np.linspace(0.0, 1.0, nfreq)
    filt = np.exp(-0.5 * (xs / 0.15) ** 2)
    kern = np.fft.irfft(filt, n=T)
    idx = (np.arange(128)[None, :, None] * 0 +
           np.arange(32)[:, None, None] * 128 +
           np.arange(128)[None, None, :] - np.arange(128)[None, :, None]) % T
    ct64 = kern[idx]
    ctiles = np.empty((32, 128, 256), ml_dtypes.bfloat16)
    ctiles[:, :, 0:128] = ct64.astype(ml_dtypes.bfloat16)
    ctiles[:, :, 128:256] = (ct64 - ctiles[:, :, 0:128].astype(np.float64)
                             ).astype(ml_dtypes.bfloat16)

    masks = np.zeros((128, 4 * 512), np.float32)
    tri = np.tril(np.full((128, 128), NEG, np.float32), -1)
    for d in range(2):
        m = np.zeros((128, 256), np.float32)
        m[:, 0:d * 128] = NEG
        m[:, d * 128:(d + 1) * 128] = tri
        masks[:, d * 256:(d + 1) * 256] = m

    ident = np.eye(128, dtype=np.float32)

    in_maps = []
    for c in range(NCORE):
        b, gidx = c // 4, c % 4
        roff = gidx * ROWS
        in_maps.append({
            "x": X[b],
            "wq": np.ascontiguousarray(Wq[:, gidx * CH:(gidx + 1) * CH]),
            "wk": np.ascontiguousarray(Wk[:, gidx * CH:(gidx + 1) * CH]),
            "wv": np.ascontiguousarray(Wv[:, gidx * CH:(gidx + 1) * CH]),
            "wout": Wout, "w_in": Win, "b_in": bin_, "w_out_b": Woutb, "b_out": bout,
            "ln_w": lnw, "ln_b": lnb, "w_ffn_in": wffn, "w_up": wup,
            "rms_w": rmsw, "lam": lam,
            "ropea": ropea,
            "ropeb": np.ascontiguousarray(ropeb_full[roff:roff + ROWS]),
            "ctiles": ctiles, "masks": masks, "ident": ident,
            "xr": np.ascontiguousarray(X[b, roff:roff + ROWS]),
        })
    return in_maps


def kernel(**inputs):
    in_maps = make_inputs(inputs)
    nc = build_nc()
    res = run_bass_kernel_spmd(nc, in_maps, core_ids=list(range(NCORE)))
    out = np.empty((B, T, D), np.float32)
    for c in range(NCORE):
        out[c // 4, (c % 4) * ROWS:(c % 4 + 1) * ROWS, :] = res.results[c]["out"]
    return out
